# version 12
# baseline (speedup 1.0000x reference)
import bisect
import numpy as np
import ml_dtypes
from contextlib import ExitStack

import concourse.bass as bass
import concourse.mybir as mybir
from concourse.bass_utils import run_bass_kernel_spmd
from concourse.bass import IndirectOffsetOnAxis
from concourse.alu_op_type import AluOpType as ALU

F32 = mybir.dt.float32
BF16 = mybir.dt.bfloat16
U8 = mybir.dt.uint8
I32 = mybir.dt.int32
AF = mybir.ActivationFunctionType
AX = mybir.AxisListType

ENGINES = ("pe", "act", "dve", "pool", "sp")
ENG_ATTR = {"pe": "tensor", "act": "scalar", "dve": "vector", "pool": "gpsimd", "sp": "sync"}
DMA_Q = {"sp": "qsp", "act": "qact", "pool": "qpool"}

HID = 1024
NH = 16
HD = 64
EPS = 1e-5
E_EXP = 8
I_MOE = 2816
I_FFN = 4096


DMA_DEPTH = 8


class _Op:
    __slots__ = ("engine", "fn", "eng_idx", "waits", "drain", "inc", "dma_sem", "dma_v")

    def __init__(self, engine, fn, eng_idx):
        self.engine = engine
        self.fn = fn
        self.eng_idx = eng_idx
        self.waits = []
        self.drain = False
        self.inc = False
        self.dma_sem = None
        self.dma_v = 0


class Prog:

    def __init__(self):
        self.eng_ops = {e: [] for e in ENGINES}
        self.writers = {}
        self.readers = {}
        self.att = {e: ([], []) for e in ENGINES}
        self.dma_n = {"qsp": 0, "qact": 0, "qpool": 0}
        self.dma_sub = {(q, i): 0 for q in ("qsp", "qact", "qpool")
                        for i in range(DMA_DEPTH)}
        self.drain_after = {e: 0 for e in ENGINES}
        self.waited = {e: {} for e in ENGINES}
        self.fence = {e: [] for e in ENGINES}
        self.last_op = {e: None for e in ENGINES}
        self.last_nondma = {e: None for e in ENGINES}
        self.n_drains = 0
        self.n_waits = 0

    def _milestone(self, op):
        e = op.engine
        idxs, vals = self.att[e]
        i = bisect.bisect_left(idxs, op.eng_idx)
        if i < len(idxs):
            return ("eng", e), vals[i]
        op.inc = True
        v = (vals[-1] if vals else 0) + 1
        idxs.append(op.eng_idx)
        vals.append(v)
        return ("eng", e), v

    def _want(self, cur, semk, v):
        if self.waited[cur.engine].get(semk, 0) >= v:
            return
        cur.waits.append((semk, v))
        self.waited[cur.engine][semk] = v
        self.n_waits += 1

    def _dep(self, cur, prod):
        if isinstance(prod, tuple):
            semk, v = prod
            self._want(cur, semk, v)
        elif prod.engine == cur.engine:
            if cur.engine == "pe":
                return
            if prod.eng_idx >= self.drain_after[cur.engine]:
                if not cur.drain:
                    cur.drain = True
                    self.n_drains += 1
                self.drain_after[cur.engine] = cur.eng_idx
        else:
            semk, v = self._milestone(prod)
            self._want(cur, semk, v)

    def op(self, engine, fn, reads=(), writes=(), dma=False, accum_write=False):
        cur = _Op(engine, fn, len(self.eng_ops[engine]))
        for semk, v in self.fence[engine]:
            self._want(cur, semk, v)
        self.fence[engine] = []
        for k in reads:
            w = self.writers.get(k)
            if w:
                for op2 in w["eng"].values():
                    self._dep(cur, op2)
                for semk, v in w["dma"].items():
                    self._dep(cur, (semk, v))
        for k in writes:
            w = self.writers.get(k)
            if w:
                for op2 in w["eng"].values():
                    self._dep(cur, op2)
                for semk, v in w["dma"].items():
                    self._dep(cur, (semk, v))
            r = self.readers.get(k)
            if r:
                for op2 in r["eng"].values():
                    self._dep(cur, op2)
                for semk, v in r["dma"].items():
                    self._dep(cur, (semk, v))
        if dma:
            q = DMA_Q[engine]
            sub = self.dma_n[q] % DMA_DEPTH
            self.dma_n[q] += 1
            self.dma_sub[(q, sub)] += 1
            cnt = self.dma_sub[(q, sub)]
            cur.dma_sem = ("dma", q, sub)
            cur.dma_v = 16 * cnt
            if cnt > 1:
                self._want(cur, cur.dma_sem, 16 * (cnt - 1))
        for k in reads:
            rd = self.readers.setdefault(k, {"eng": {}, "dma": {}})
            if dma:
                rd["dma"][cur.dma_sem] = cur.dma_v
            else:
                rd["eng"][engine] = cur
        for k in writes:
            if accum_write:
                w = self.writers.setdefault(k, {"eng": {}, "dma": {}})
                if dma:
                    w["dma"][cur.dma_sem] = cur.dma_v
                else:
                    w["eng"][engine] = cur
            elif dma:
                self.writers[k] = {"eng": {}, "dma": {cur.dma_sem: cur.dma_v}}
                self.readers[k] = {"eng": {}, "dma": {}}
            else:
                self.writers[k] = {"eng": {engine: cur}, "dma": {}}
                self.readers[k] = {"eng": {}, "dma": {}}
        self.eng_ops[engine].append(cur)
        self.last_op[engine] = cur
        if not dma:
            self.last_nondma[engine] = cur
        return cur

    def barrier(self):
        pend = []
        for e in ENGINES:
            if self.last_nondma[e] is not None:
                semk, v = self._milestone(self.last_nondma[e])
                pend.append((semk, v))
        for (q, sub), n in self.dma_sub.items():
            if n:
                pend.append((("dma", q, sub), 16 * n))
        for e in ENGINES:
            self.fence[e] = list(pend)

    def emit(self, block, sems):
        for e in ENGINES:
            ops = self.eng_ops[e]

            def body(eng, ops=ops, e=e):
                nc = eng.bass
                with nc.allow_low_precision(reason="bf16 operands per problem tolerance"):
                    for op in ops:
                        for semk, v in op.waits:
                            eng.wait_ge(sems[semk], v)
                        if op.drain:
                            eng.drain()
                        instr = op.fn(eng)
                        if op.dma_sem:
                            instr.then_inc(sems[op.dma_sem], 16)
                        elif op.inc:
                            instr.then_inc(sems[("eng", e)], 1)

            getattr(block, ENG_ATTR[e])(body)


class Arena:

    def __init__(self, nc, ctx, total_bytes):
        self.nc = nc
        self.handle = ctx.enter_context(
            nc.sbuf_tensor("arena", [128, total_bytes], U8))
        self.base = nc.lookup_mloc(self.handle).addr
        self.total = total_bytes

    def place(self, name, shape, dtype, offset):
        per_part = int(np.prod(shape[1:])) * mybir.dt.size(dtype)
        assert offset + per_part <= self.total, \
            f"{name}: {offset}+{per_part} > {self.total}"
        assert offset % 32 == 0, name
        return self.nc.alloc_sbuf_tensor_at(name, list(shape), dtype,
                                            offset=self.base + offset)


def windows_for(S):
    out = []
    for i in range(0, S, 256):
        out.append((max(0, i - 128), min(S, i + 384)))
    return out


def alpha_table(S):
    wins = windows_for(S)
    nw = len(wins)
    al = np.arange(256, dtype=np.float64) / 255.0
    row = np.zeros(512 * nw, dtype=np.float64)
    for w, (start, end) in enumerate(wins):
        L = end - start
        ov = 0 if w == 0 else 256
        for c in range(L):
            if c < ov:
                val = al[c]
            elif w < nw - 1 and c >= L - 256:
                val = 1.0 - al[c - (L - 256)]
            else:
                val = 1.0
            row[512 * w + c] = val
    out = np.broadcast_to(row.astype(np.float32), (128, 512 * nw))
    return np.ascontiguousarray(out.astype(ml_dtypes.bfloat16))


def build_program(S=2048, NSEQ=2, taps=(), moe_sparse=True, CAP=1024):
    nc = bass.Bass("TRN2", target_bir_lowering=False)
    NT = S // 128
    NTB = S // 512
    wins = windows_for(S)
    NW = len(wins)
    NST = CAP // 128
    TBW = min(512, CAP)
    NTBC = CAP // TBW
    NIH = I_MOE // 256
    scale = float(HD) ** (-0.5)

    dr = {}
    def din(name, shape, dt):
        dr[name] = nc.dram_tensor(name, shape, dt, kind="ExternalInput").ap()
        return dr[name]

    x = din("x", [NSEQ, S, HID], F32)
    y = nc.dram_tensor("y", [NSEQ, S, HID], F32, kind="ExternalOutput").ap()
    wq = din("wq", [HID, HID], F32)
    wk = din("wk", [HID, HID], F32)
    wv = din("wv", [HID, HID], F32)
    wproj = din("wproj", [HID, HID], F32)
    w1 = din("w1", [HID, I_FFN], F32)
    w2 = din("w2", [I_FFN, HID], F32)
    router_t = din("router_t", [HID, E_EXP], F32)
    wg_t = din("wg_t", [E_EXP, HID, I_MOE], F32)
    wu_t = din("wu_t", [E_EXP, HID, I_MOE], F32)
    wd_t = din("wd_t", [E_EXP, I_MOE, HID], F32)
    d_ident = din("c_ident", [128, 128], BF16)
    d_ones = din("c_ones", [128, 128], BF16)
    d_alpha = din("c_alpha", [128, 512 * NW], BF16)
    d_ln1g = din("c_ln1g", [128, HID], F32)
    d_ln1b = din("c_ln1b", [128, HID], F32)
    d_ln2g = din("c_ln2g", [128, HID], F32)
    d_ln2b = din("c_ln2b", [128, HID], F32)
    d_bv = din("c_bv", [128, HID], F32)
    d_bproj = din("c_bproj", [128, HID], F32)
    d_b2 = din("c_b2", [128, HID], F32)
    d_bq = din("c_bq", [128, 8], F32)
    d_bk = din("c_bk", [128, 8], F32)
    d_b1 = din("c_b1", [128, 32], F32)
    d_tri = din("c_tri", [128, 128], BF16)
    d_onesf = din("c_onesf", [128, 128], F32)
    xg_hbm = nc.dram_tensor("xg_hbm", [E_EXP, CAP, HID], BF16, kind="Internal").ap()
    yb_hbm = nc.dram_tensor("yb_hbm", [E_EXP, CAP, HID], F32, kind="Internal").ap()

    tap_out = {}
    for tname, tshape, tdt in taps:
        tap_out[tname] = nc.dram_tensor("tap_" + tname, list(tshape), tdt,
                                        kind="ExternalOutput").ap()

    P = Prog()
    ctx = ExitStack()
    with ctx:
        KB = 1024
        ar = Arena(nc, ctx, 204 * KB)
        PB = 196 * KB
        ident = ar.place("ident", [128, 128], BF16, PB)
        ones = ar.place("ones", [128, 128], BF16, PB + 256)
        c_bq = ar.place("c_bq", [128, 8], F32, PB + 512)
        c_bk = ar.place("c_bk", [128, 8], F32, PB + 544)
        c_b1 = ar.place("c_b1", [128, 32], F32, PB + 576)
        c_rt = ar.place("c_rt", [128, 64], BF16, PB + 704)
        stats = ar.place("stats", [128, 16], F32, PB + 960)
        statb = ar.place("statb", [128, 80], F32, PB + 2208)
        c_sb = ar.place("c_sb", [128, NT * 8], F32, PB + 1024)
        lgt = ar.place("lgt", [128, 8], F32, PB + 2048)
        eq1 = ar.place("eq1", [128, 8], F32, PB + 2080)
        eq2 = ar.place("eq2", [128, 8], F32, PB + 2112)
        lmsk = ar.place("lmsk", [128, 8], F32, PB + 2144)
        t1t = ar.place("t1t", [128, 8], F32, PB + 2176)
        W0, W1, W3, W4, W5 = 0, 32 * KB, 96 * KB, 128 * KB, 160 * KB

        psb = [ctx.enter_context(nc.psum_tensor(f"pb{i}", [128, 512], F32))
               for i in range(7)]
        pbT = ctx.enter_context(nc.psum_tensor("pbT", [128, 1024], BF16))

        sems = {}
        for e in ENGINES:
            sems[("eng", e)] = ctx.enter_context(nc.semaphore(f"sem_{e}"))
        for q in ("qsp", "qact", "qpool"):
            for i in range(DMA_DEPTH):
                sems[("dma", q, i)] = ctx.enter_context(nc.semaphore(f"sem_{q}{i}"))

        _bcreg = {}

        def bc_reg(g):
            if "r" not in _bcreg:
                _bcreg["r"] = g.to_reg(CAP - 1)
            return _bcreg["r"]

        def ldconst(dst, src, key):
            P.op("sp", lambda s, d=dst, sr=src: s.dma_start(out=d[:], in_=sr),
                 writes=[key], dma=True)

        def tap(name, ap, keys):
            if name in tap_out:
                P.op("sp", lambda s, a=ap, o=tap_out[name]: s.dma_start(out=o, in_=a),
                     reads=list(keys), dma=True)

        def ln_pass1(tile_ap, tile_keys, sq_scr, sq_key, col, tag):
            P.op("dve", lambda v, a=tile_ap, col=col: v.tensor_reduce(
                statb[:, col:col + 1], a, AX.X, ALU.add),
                reads=tile_keys, writes=[f"{tag}.sx{col}"])
            P.op("act", lambda sc, a=tile_ap, col=col: sc.activation(
                sq_scr[:], a, AF.Square, accum_out=statb[:, 16 + col:17 + col]),
                reads=tile_keys, writes=[sq_key, f"{tag}.sq{col}"])

        def ln_finish(n, tag):
            sxk = [f"{tag}.sx{c}" for c in range(n)]
            sqk = [f"{tag}.sq{c}" for c in range(n)]
            P.op("dve", lambda v, n=n: v.tensor_scalar_mul(
                statb[:, 32:32 + n], statb[:, 0:n], 1.0 / HID),
                reads=sxk, writes=[f"{tag}.m"])
            P.op("dve", lambda v, n=n: v.tensor_scalar(
                statb[:, 48:48 + n], statb[:, 16:16 + n], 1.0 / HID, EPS,
                ALU.mult, ALU.add), reads=sqk, writes=[f"{tag}.v"])
            P.op("act", lambda sc, n=n: sc.activation(
                statb[:, 64:64 + n], statb[:, 32:32 + n], AF.Square),
                reads=[f"{tag}.m"], writes=[f"{tag}.mm"])
            P.op("dve", lambda v, n=n: v.tensor_sub(
                statb[:, 48:48 + n], statb[:, 48:48 + n], statb[:, 64:64 + n]),
                reads=[f"{tag}.v", f"{tag}.mm"], writes=[f"{tag}.v"])
            P.op("act", lambda sc, n=n: sc.activation(
                statb[:, 64:64 + n], statb[:, 48:48 + n], AF.Ln),
                reads=[f"{tag}.v"], writes=[f"{tag}.l"])
            P.op("act", lambda sc, n=n: sc.activation(
                statb[:, 48:48 + n], statb[:, 64:64 + n], AF.Exp, scale=-0.5),
                reads=[f"{tag}.l"], writes=[f"{tag}.r"])

        def ln_mr(col):
            return statb[:, 32 + col:33 + col], statb[:, 48 + col:49 + col]

        xnT = ar.place("xnT", [128, 8 * S], BF16, W0)
        qT = ar.place("qT", [128, 8 * S], BF16, W1)
        kT = ar.place("kT", [128, 8 * S], BF16, W1 + 32 * KB)
        v_tm = ar.place("v_tm", [128, NT * 1024], BF16, W3)
        x0t = [ar.place(f"x0t{b}", [128, HID], F32, W5 + b * 4 * KB) for b in range(4)]
        xn_tmp = [ar.place(f"xnt{b}", [128, HID], BF16, W5 + 16 * KB + b * 2 * KB)
                  for b in range(2)]
        sq_scr = ar.place("sq", [128, HID], F32, W5 + 20 * KB)
        c_ln1g = ar.place("ln1g", [128, HID], F32, W5 + 24 * KB)
        c_ln1b = ar.place("ln1b", [128, HID], F32, W5 + 28 * KB)
        c_bv = ar.place("bv", [128, HID], F32, W5 + 32 * KB)
        wbuf = [ar.place(f"wbuf{b}", [128, 8 * 1024], BF16, W4 + b * 16 * KB)
                for b in range(2)]
        zsrc_f32 = ar.place("zsrcf", [128, NST * HID], F32, W5)
        ctxT = ar.place("ctx", [128, 8 * S], BF16, W0)
        expE = ar.place("expE", [128, 2048], BF16, W4)
        expO = ar.place("expO", [128, 2048], BF16, W4 + 4 * KB)
        wrow = ar.place("wrow", [128, 512], BF16, W4 + 8 * KB)
        wb_sb = ar.place("wbsb", [128, 512], F32, W4 + 9 * KB)
        tmp256 = ar.place("tmp256", [128, 256], BF16, W4 + 11 * KB)
        c_alpha = ar.place("alpha", [128, 512 * NW], BF16, W4 + 12 * KB)
        lnr = ar.place("lnr", [128, 512], F32, W4 + 12 * KB + ((512 * NW * 2 + 1023) // 1024) * 1024)
        rS = ar.place("rS", [128, NT * 1024], F32, W1)
        o_bf = ar.place("obf", [128, NT * 1024], BF16, W3)
        wbufP = ar.place("wbufP", [128, 8 * 1024], BF16, W4)
        x0t2 = [ar.place(f"x0u{b}", [128, HID], F32, W5 + b * 4 * KB) for b in range(2)]
        c_bproj = ar.place("bproj", [128, HID], F32, W5 + 8 * KB)
        c_b2c = ar.place("b2c", [128, HID], F32, W5 + 12 * KB)
        c_pb2 = ar.place("pb2", [128, HID], F32, W5 + 16 * KB)
        xbA = [ar.place(f"xbA{b}", [128, HID], F32, W5 + 20 * KB + b * 4 * KB)
               for b in range(2)]
        xn2T = ar.place("xn2T", [128, 8 * 512], BF16, W0)
        w1buf = [ar.place(f"w1b{b}", [128, 8 * 512], BF16, W0 + 8 * KB + b * 8 * KB)
                 for b in range(2)]
        w2buf = [ar.place(f"w2b{b}", [128, 4 * 512], BF16, W0 + 24 * KB + b * 4 * KB)
                 for b in range(2)]
        h1T = ar.place("h1T", [128, 32 * 512], BF16, W4)
        xn2tm = ar.place("xn2tm", [128, HID], BF16, W5)
        sq2 = ar.place("sq2", [128, HID], F32, W5 + 4 * KB)
        c_ln2g = ar.place("ln2g", [128, HID], F32, W5 + 8 * KB)
        c_ln2b = ar.place("ln2b", [128, HID], F32, W5 + 12 * KB)
        x1T = ar.place("x1T", [128, 8 * S], BF16, W3)
        x1tmp = [ar.place(f"x1t{b}", [128, HID], BF16, W5 + b * 2 * KB) for b in range(2)]
        sq3 = ar.place("sq3", [128, HID], F32, W5 + 4 * KB)
        wd_sb = ar.place("wd", [128, NIH * 1024], BF16, W0)
        if moe_sparse:
            HB = W4 + 54 * KB
            NWB = 2
        else:
            HB = W4 + 44 * KB
            NWB = 4
        wgbuf = [ar.place(f"wgb{b}", [128, 8 * 128], BF16, HB + b * 2 * KB)
                 for b in range(NWB)]
        wubuf = [ar.place(f"wub{b}", [128, 8 * 128], BF16, HB + NWB * 2 * KB + b * 2 * KB)
                 for b in range(NWB)]
        silu_t = [ar.place(f"silu{b}", [128, 512], F32, HB + NWB * 4 * KB + b * 2 * KB)
                  for b in range(2)]
        h_half = ar.place("hh", [128, NIH * (CAP if moe_sparse else S)], BF16, W4)
        outb = [ar.place(f"outb{b}", [128, HID], F32, W0 + b * 4 * KB) for b in range(2)]
        sq4 = ar.place("sq4", [128, HID], F32, W0 + 8 * KB)
        sloti_sb = ar.place("sloti", [128, 8 * NT], I32, PB + 2560)
        maskt = ar.place("maskt", [128, 8], F32, PB + 3104)
        maskb = ar.place("maskb", [128, 8], BF16, PB + 3072)
        baserow = ar.place("baserow", [128, 8], F32, PB + 3136)
        slotf = ar.place("slotf", [128, 8], F32, PB + 3168)
        okt = ar.place("okt", [128, 8], F32, PB + 3200)
        c_tri = ar.place("c_tri", [128, 128], BF16, PB + 3232)
        c_onesf = ar.place("c_onesf", [128, 128], F32, PB + 3744)
        xsrc = [ar.place(f"xsrc{b}", [128, HID], BF16, W5 + 8 * KB + b * 2 * KB)
                for b in range(2)]
        xgT = ar.place("xgT", [128, 8 * CAP], BF16, W3)
        ygbuf = [ar.place(f"yg{b}", [128, HID], F32, W3 + 16 * KB + b * 4 * KB)
                 for b in range(2)]
        xgt = [ar.place(f"xgt{b}", [128, HID], BF16, W0 + 22 * KB + b * 2 * KB)
               for b in range(2)]
        ysum = ar.place("ysum", [128, NST * HID], F32, W4 + 22 * KB)

        for s in range(NSEQ):
            sfx = f".s{s}"


            if s == 0:
                ldconst(ident, d_ident, "ident")
                ldconst(ones, d_ones, "ones")
                ldconst(c_bq, d_bq, "c_bq")
                ldconst(c_bk, d_bk, "c_bk")
                ldconst(c_b1, d_b1, "c_b1")
                if moe_sparse:
                    ldconst(c_tri, d_tri, "c_tri")
                    ldconst(c_onesf, d_onesf, "c_onesf")
                P.op("pool", lambda g: g.dma_start(
                    out=c_rt[:].rearrange("p (kk e) -> p kk e", kk=8),
                    in_=router_t.rearrange("(kk p) e -> p kk e", p=128)),
                    writes=["c_rt"], dma=True)
            ldconst(c_ln1g, d_ln1g, "ln1g")
            ldconst(c_ln1b, d_ln1b, "ln1b")
            ldconst(c_bv, d_bv, "c_bv")

            for tt in range(NT):
                b = tt % 4
                P.op("sp", lambda sp, b=b, tt=tt, s=s: sp.dma_start(
                    out=x0t[b][:], in_=x[s, 128 * tt:128 * tt + 128, :]),
                    writes=[f"x0t{b}"], dma=True)
                ln_pass1(x0t[b][:], [f"x0t{b}"], sq_scr, "sq", tt, "A")
            ln_finish(NT, "A")
            for tt in range(NT):
                bx = tt % 4
                b = tt % 2
                P.op("sp", lambda sp, bx=bx, tt=tt, s=s: sp.dma_start(
                    out=x0t[bx][:], in_=x[s, 128 * tt:128 * tt + 128, :]),
                    writes=[f"x0t{bx}"], dma=True)
                mcol, rcol = ln_mr(tt)
                P.op("dve", lambda v, bx=bx, mcol=mcol, rcol=rcol: v.tensor_scalar(
                    sq_scr[:], x0t[bx][:], mcol, rcol,
                    ALU.subtract, ALU.mult),
                    reads=[f"x0t{bx}", "A.m", "A.r"], writes=["sq"])
                P.op("dve", lambda v: v.tensor_tensor(
                    sq_scr[:], sq_scr[:], c_ln1g[:], ALU.mult),
                    reads=["sq", "ln1g"], writes=["sq"])
                P.op("dve", lambda v, b=b: v.tensor_tensor(
                    xn_tmp[b][:], sq_scr[:], c_ln1b[:], ALU.add),
                    reads=["sq", "ln1b"], writes=[f"xnt{b}"])
                for j in range(8):
                    P.op("pe", lambda t, b=b, j=j: t.transpose(
                        pbT[:, 128 * j:128 * j + 128],
                        xn_tmp[b][:, 128 * j:128 * j + 128], ident[:]),
                        reads=[f"xnt{b}", "ident"], writes=["pbT"])
                P.op("dve", lambda v, tt=tt: v.tensor_copy(
                    xnT[:].rearrange("p (j t) -> p j t", j=8)[:, :, 128 * tt:128 * tt + 128],
                    pbT[:].rearrange("p (j t) -> p j t", j=8)),
                    reads=["pbT"], writes=[f"xnT.{tt}"])
            tap("xn" + sfx, xnT[:], [f"xnT.{tt}" for tt in range(NT)])


            def ld_w1024(dst, src, key):
                P.op("pool", lambda g, d=dst, sr=src: g.dma_start(
                    out=d[:].rearrange("p (kk o) -> p kk o", kk=8),
                    in_=sr.rearrange("(kk p) o -> p kk o", p=128)),
                    writes=[key], dma=True)

            ld_w1024(wbuf[0], wq, "wbuf0")
            ld_w1024(wbuf[1], wk, "wbuf1")

            pbi = [0]
            def nxpb(n=4):
                pbi[0] = (pbi[0] + 1) % n
                return pbi[0]

            def qkv_fm(dst, dstkey, wb, wkey, bias_t, bias_key):
                for j in range(8):
                    for tb in range(NTB):
                        pk = nxpb()
                        for kk in range(8):
                            P.op("pe", lambda t, j=j, tb=tb, kk=kk, pk=pk, wb=wb:
                                 t.matmul(psb[pk][:],
                                          wb[:, 1024 * kk + 128 * j:1024 * kk + 128 * j + 128],
                                          xnT[:, S * kk + 512 * tb:S * kk + 512 * tb + 512],
                                          start=(kk == 0), stop=(kk == 7)),
                                 reads=[wkey] + [f"xnT.{4 * tb + i}" for i in range(4)],
                                 writes=[f"pb{pk}"])
                        P.op("act", lambda sc, j=j, tb=tb, pk=pk, dst=dst, bias_t=bias_t:
                             sc.activation(dst[:, S * j + 512 * tb:S * j + 512 * tb + 512],
                                           psb[pk][:], AF.Identity, bias=bias_t[:, j:j + 1]),
                             reads=[f"pb{pk}", bias_key], writes=[f"{dstkey}.{j}.{tb}"])

            qkv_fm(qT, "qT", wbuf[0], "wbuf0", c_bq, "c_bq")
            ld_w1024(wbuf[0], wv, "wbuf0")
            qkv_fm(kT, "kT", wbuf[1], "wbuf1", c_bk, "c_bk")
            for tt in range(NT):
                for ob in range(2):
                    pk = nxpb()
                    for kk in range(8):
                        P.op("pe", lambda t, tt=tt, ob=ob, kk=kk, pk=pk:
                             t.matmul(psb[pk][:],
                                      xnT[:, S * kk + 128 * tt:S * kk + 128 * tt + 128],
                                      wbuf[0][:, 1024 * kk + 512 * ob:1024 * kk + 512 * ob + 512],
                                      start=(kk == 0), stop=(kk == 7)),
                             reads=["wbuf0", f"xnT.{tt}"], writes=[f"pb{pk}"])
                    P.op("dve", lambda v, tt=tt, ob=ob, pk=pk: v.tensor_tensor(
                        v_tm[:, 1024 * tt + 512 * ob:1024 * tt + 512 * ob + 512],
                        psb[pk][:], c_bv[:, 512 * ob:512 * ob + 512], ALU.add),
                        reads=[f"pb{pk}", "c_bv"], writes=[f"v.{tt}"])
            tap("q" + sfx, qT[:], [f"qT.{j}.{b}" for j in range(8) for b in range(NTB)])
            tap("k" + sfx, kT[:], [f"kT.{j}.{b}" for j in range(8) for b in range(NTB)])
            tap("v" + sfx, v_tm[:], [f"v.{tt}" for tt in range(NT)])

            P.barrier()
            if moe_sparse and s == 0:
                zf32 = zsrc_f32
                P.op("dve", lambda v, zf32=zf32: v.memset(zf32[:], 0.0),
                     writes=["zsrc"])
                for e0 in range(E_EXP):
                    P.op("act", lambda sc, e0=e0, zf32=zf32: sc.dma_start(
                        out=yb_hbm[e0].rearrange("(n p) h -> p n h", p=128),
                        in_=zf32[:].rearrange("p (n h) -> p n h", n=NST)),
                        reads=["zsrc"],
                        writes=[f"yb.{e0}.{st0}" for st0 in range(NST)],
                        dma=True)
                    P.op("pool", lambda g, e0=e0, zf32=zf32: g.dma_start(
                        out=xg_hbm[e0].rearrange("(n p) h -> p n h", p=128),
                        in_=zf32[:].rearrange("p (n h) -> p n h", n=NST)),
                        reads=["zsrc"], writes=[f"xgh.{e0}"],
                        dma=True, accum_write=True)
            ldconst(c_alpha, d_alpha, "alpha")

            for w, (start, end) in enumerate(wins):
                L = end - start
                nkt = (L + 127) // 128
                ov = 0 if w == 0 else 256
                tbs = [f"qT.{0}.{b}" for b in range(start // 512, (end - 1) // 512 + 1)]
                for hp in range(8):
                    qkeys = [f"qT.{hp}.{b}" for b in range(start // 512, (end - 1) // 512 + 1)]
                    kkeys = [f"kT.{hp}.{b}" for b in range(start // 512, (end - 1) // 512 + 1)]
                    for kt in range(nkt):
                        ktL = min(128, L - 128 * kt)
                        pk = kt % 2
                        po = 2 + kt % 2
                        c0 = S * hp + start
                        P.op("pe", lambda t, kt=kt, ktL=ktL, pk=pk, c0=c0, L=L:
                             t.matmul(psb[pk][0:ktL, 0:L],
                                      kT[0:64, c0 + 128 * kt:c0 + 128 * kt + ktL],
                                      qT[0:64, c0:c0 + L], start=True, stop=True),
                             reads=kkeys + qkeys, writes=[f"pb{pk}"])
                        P.op("pe", lambda t, kt=kt, ktL=ktL, po=po, c0=c0, L=L:
                             t.matmul(psb[po][0:ktL, 0:L],
                                      kT[64:128, c0 + 128 * kt:c0 + 128 * kt + ktL],
                                      qT[64:128, c0:c0 + L], start=True, stop=True),
                             reads=kkeys + qkeys, writes=[f"pb{po}"])
                        P.op("act", lambda sc, kt=kt, ktL=ktL, pk=pk, L=L:
                             sc.activation(expE[0:ktL, 512 * kt:512 * kt + L],
                                           psb[pk][0:ktL, 0:L], AF.Exp, scale=scale),
                             reads=[f"pb{pk}"], writes=[f"expE.{kt}"])
                        P.op("act", lambda sc, kt=kt, ktL=ktL, po=po, L=L:
                             sc.activation(expO[0:ktL, 512 * kt:512 * kt + L],
                                           psb[po][0:ktL, 0:L], AF.Exp, scale=scale),
                             reads=[f"pb{po}"], writes=[f"expO.{kt}"])
                    for par, expT, ekey in ((0, expE, "expE"), (1, expO, "expO")):
                        for kt in range(nkt):
                            ktL = min(128, L - 128 * kt)
                            tti = (start + 128 * kt) // 128
                            hcol = 1024 * tti + 64 * (2 * hp + par)
                            tp = (0, 64 * par)
                            P.op("pe", lambda t, kt=kt, ktL=ktL, tti=tti, hcol=hcol,
                                 expT=expT, par=par, L=L, nkt=nkt, tp=tp:
                                 t.matmul(psb[4][64 * par:64 * par + 64, 0:L],
                                          v_tm[0:ktL, hcol:hcol + 64],
                                          expT[0:ktL, 512 * kt:512 * kt + L],
                                          start=(kt == 0), stop=(kt == nkt - 1),
                                          tile_position=tp),
                                 reads=[f"v.{tti}", f"{ekey}.{kt}"], writes=["pb4"])
                    for par, expT, ekey in ((0, expE, "expE"), (1, expO, "expO")):
                        for kt in range(nkt):
                            ktL = min(128, L - 128 * kt)
                            tp = (0, 32 * par)
                            P.op("pe", lambda t, kt=kt, ktL=ktL, par=par, expT=expT,
                                 L=L, nkt=nkt, tp=tp:
                                 t.matmul(psb[5][32 * par:32 * par + 1, 0:L],
                                          ones[0:ktL, 0:1],
                                          expT[0:ktL, 512 * kt:512 * kt + L],
                                          start=(kt == 0), stop=(kt == nkt - 1),
                                          tile_position=tp),
                                 reads=["ones", f"{ekey}.{kt}"], writes=["pb5"])
                    for par in range(2):
                        r = 32 * par
                        P.op("act", lambda sc, r=r, L=L: sc.activation(
                            lnr[r:r + 1, 0:L], psb[5][r:r + 1, 0:L], AF.Ln),
                            reads=["pb5"], writes=[f"lnr{par}"])
                    for par in range(2):
                        r = 32 * par
                        P.op("act", lambda sc, r=r, L=L: sc.activation(
                            wrow[r:r + 1, 0:L], lnr[r:r + 1, 0:L], AF.Exp,
                            scale=-1.0), reads=[f"lnr{par}"], writes=[f"wrow{par}"])
                    for par in range(2):
                        r = 32 * par
                        P.op("dve", lambda v, r=r, L=L, w=w: v.tensor_tensor(
                            wrow[r:r + 1, 0:L], wrow[r:r + 1, 0:L],
                            c_alpha[r:r + 1, 512 * w:512 * w + L], ALU.mult),
                            reads=[f"wrow{par}", "alpha"], writes=[f"wrow{par}"])
                    for par in range(2):
                        r = 32 * par
                        tp = (32 * par, 64 * par)
                        P.op("pe", lambda t, r=r, L=L, par=par, tp=tp: t.matmul(
                            psb[6][64 * par:64 * par + 64, 0:L],
                            ones[r:r + 1, 0:64], wrow[r:r + 1, 0:L],
                            start=True, stop=True, tile_position=tp),
                            reads=["ones", f"wrow{par}"], writes=["pb6"])
                    P.op("dve", lambda v, L=L: v.tensor_copy(
                        wb_sb[:, 0:L], psb[6][:, 0:L]),
                        reads=["pb6"], writes=["wb"])
                    cc = S * hp + start
                    P.op("dve", lambda v, cc=cc, ov=ov, L=L: v.tensor_tensor(
                        ctxT[:, cc + ov:cc + L], psb[4][:, ov:L],
                        wb_sb[:, ov:L], ALU.mult),
                        reads=["pb4", "wb"], writes=[f"ctx.{hp}.{w}"])
                    if w > 0:
                        P.op("dve", lambda v, cc=cc: v.tensor_tensor(
                            tmp256[:], psb[4][:, 0:256], wb_sb[:, 0:256], ALU.mult),
                            reads=["pb4", "wb"], writes=["tmp256"])
                        P.op("dve", lambda v, cc=cc: v.tensor_tensor(
                            ctxT[:, cc:cc + 256], ctxT[:, cc:cc + 256],
                            tmp256[:], ALU.add),
                            reads=["tmp256", f"ctx.{hp}.{w - 1}"],
                            writes=[f"ctx.{hp}.{w}", f"ctx.{hp}.{w - 1}"])
            ctx_keys = [f"ctx.{hp}.{w}" for hp in range(8) for w in range(NW)]
            tap("ctx" + sfx, ctxT[:], ctx_keys)

            P.barrier()
            ld_w1024(wbufP, wproj, "wbufP")
            ldconst(c_bproj, d_bproj, "bproj")
            ldconst(c_b2c, d_b2, "b2c")
            P.op("dve", lambda v: v.tensor_add(c_pb2[:], c_bproj[:], c_b2c[:]),
                 reads=["bproj", "b2c"], writes=["pb2"])
            for tt in range(NT):
                b = tt % 2
                P.op("act", lambda sc, b=b, tt=tt, s=s: sc.dma_start(
                    out=x0t2[b][:], in_=x[s, 128 * tt:128 * tt + 128, :]),
                    writes=[f"x0u{b}"], dma=True)
                P.op("dve", lambda v, b=b: v.tensor_add(
                    xbA[b][:], x0t2[b][:], c_bproj[:]),
                    reads=[f"x0u{b}", "bproj"], writes=[f"xbA{b}"])
                P.op("dve", lambda v, b=b: v.scalar_tensor_tensor(
                    x0t2[b][:], x0t2[b][:], 2.0, c_pb2[:], ALU.mult, ALU.add),
                    reads=[f"x0u{b}", "pb2"], writes=[f"x0u{b}"])
                for ob in range(2):
                    pk = nxpb(2)
                    for kk in range(8):
                        P.op("pe", lambda t, tt=tt, ob=ob, kk=kk, pk=pk:
                             t.matmul(psb[pk][:],
                                      ctxT[:, S * kk + 128 * tt:S * kk + 128 * tt + 128],
                                      wbufP[:, 1024 * kk + 512 * ob:1024 * kk + 512 * ob + 512],
                                      start=(kk == 0), stop=(kk == 7)),
                             reads=["wbufP", f"ctx.{kk}.{NW - 1}", f"ctx.{kk}.{NW - 2}"],
                             writes=[f"pb{pk}"])
                    P.op("dve", lambda v, tt=tt, ob=ob, pk=pk, b=b: v.tensor_tensor(
                        rS[:, 1024 * tt + 512 * ob:1024 * tt + 512 * ob + 512],
                        psb[pk][:], x0t2[b][:, 512 * ob:512 * ob + 512], ALU.add),
                        reads=[f"pb{pk}", f"x0u{b}"], writes=[f"rS.{tt}.{ob}"])
                    P.op("dve", lambda v, tt=tt, ob=ob, pk=pk, b=b: v.tensor_tensor(
                        o_bf[:, 1024 * tt + 512 * ob:1024 * tt + 512 * ob + 512],
                        psb[pk][:], xbA[b][:, 512 * ob:512 * ob + 512], ALU.add),
                        reads=[f"pb{pk}", f"xbA{b}"], writes=[f"obf.{tt}"])
            tap("o" + sfx, o_bf[:], [f"obf.{tt}" for tt in range(NT)])

            P.barrier()
            ldconst(c_ln2g, d_ln2g, "ln2g")
            ldconst(c_ln2b, d_ln2b, "ln2b")

            for tb in range(NTB):
                for ti in range(4):
                    tt = 4 * tb + ti
                    ln_pass1(o_bf[:, 1024 * tt:1024 * tt + 1024], [f"obf.{tt}"],
                             sq2, "sq2", ti, "E")
                ln_finish(4, "E")
                for ti in range(4):
                    tt = 4 * tb + ti
                    okey = [f"obf.{tt}"]
                    oap = o_bf[:, 1024 * tt:1024 * tt + 1024]
                    mcol, rcol = ln_mr(ti)
                    P.op("dve", lambda v, oap=oap, mcol=mcol, rcol=rcol: v.tensor_scalar(
                        sq2[:], oap, mcol, rcol,
                        ALU.subtract, ALU.mult),
                        reads=okey + ["E.m", "E.r"], writes=["sq2"])
                    P.op("dve", lambda v: v.tensor_tensor(
                        sq2[:], sq2[:], c_ln2g[:], ALU.mult),
                        reads=["sq2", "ln2g"], writes=["sq2"])
                    P.op("dve", lambda v: v.tensor_tensor(
                        xn2tm[:], sq2[:], c_ln2b[:], ALU.add),
                        reads=["sq2", "ln2b"], writes=["xn2tm"])
                    for j in range(8):
                        P.op("pe", lambda t, j=j: t.transpose(
                            pbT[:, 128 * j:128 * j + 128],
                            xn2tm[:, 128 * j:128 * j + 128], ident[:]),
                            reads=["xn2tm", "ident"], writes=["pbT"])
                    P.op("dve", lambda v, ti=ti: v.tensor_copy(
                        xn2T[:].rearrange("p (j t) -> p j t", j=8)[:, :, 128 * ti:128 * ti + 128],
                        pbT[:].rearrange("p (j t) -> p j t", j=8)),
                        reads=["pbT"], writes=[f"xn2T.{ti}"])
                for c in range(8):
                    wkey = f"w1c.{c % 2}"
                    P.op("pool", lambda g, c=c, buf=w1buf[c % 2]: g.dma_start(
                        out=buf[:].rearrange("p (kk i) -> p kk i", kk=8),
                        in_=w1.rearrange("(kk p) i -> p kk i", p=128)[:, :, 512 * c:512 * c + 512]),
                        writes=[wkey], dma=True)
                    for i4 in range(4):
                        ii = 4 * c + i4
                        pk = nxpb(4)
                        for kk in range(8):
                            P.op("pe", lambda t, kk=kk, i4=i4, pk=pk, buf=w1buf[c % 2]:
                                 t.matmul(psb[pk][:],
                                          buf[:, 512 * kk + 128 * i4:512 * kk + 128 * i4 + 128],
                                          xn2T[:, 512 * kk:512 * kk + 512],
                                          start=(kk == 0), stop=(kk == 7)),
                                 reads=[wkey] + [f"xn2T.{i}" for i in range(4)],
                                 writes=[f"pb{pk}"])
                        P.op("act", lambda sc, ii=ii, pk=pk: sc.activation(
                            h1T[:, 512 * ii:512 * ii + 512], psb[pk][:], AF.Gelu,
                            bias=c_b1[:, ii:ii + 1]),
                            reads=[f"pb{pk}", "c_b1"], writes=[f"h1.{ii}"])
                if NTB == 1:
                    tap("xn2" + sfx, xn2T[:], [f"xn2T.{i}" for i in range(4)])
                    tap("h1" + sfx, h1T[:], [f"h1.{ii}" for ii in range(32)])
                for oh in range(2):
                    for a in range(8):
                        wkey = f"w2c.{a % 2}"
                        P.op("pool", lambda g, a=a, oh=oh, buf=w2buf[a % 2]: g.dma_start(
                            out=buf[:].rearrange("p (kk o) -> p kk o", kk=4),
                            in_=w2.rearrange("(kk p) o -> p kk o", p=128)[:, 4 * a:4 * a + 4, 512 * oh:512 * oh + 512]),
                            writes=[wkey], dma=True)
                        for ti in range(4):
                            for k4 in range(4):
                                kk = 4 * a + k4
                                P.op("pe", lambda t, kk=kk, k4=k4, ti=ti, a=a,
                                     buf=w2buf[a % 2]:
                                     t.matmul(psb[ti][:],
                                              h1T[:, 512 * kk + 128 * ti:512 * kk + 128 * ti + 128],
                                              buf[:, 512 * k4:512 * k4 + 512],
                                              start=(kk == 0), stop=(kk == 31)),
                                     reads=[wkey, f"h1.{kk}"], writes=[f"pb{ti}"])
                    for ti in range(4):
                        tt = 4 * tb + ti
                        P.op("dve", lambda v, tt=tt, oh=oh, ti=ti: v.tensor_tensor(
                            rS[:, 1024 * tt + 512 * oh:1024 * tt + 512 * oh + 512],
                            rS[:, 1024 * tt + 512 * oh:1024 * tt + 512 * oh + 512],
                            psb[ti][:], ALU.add),
                            reads=[f"pb{ti}", f"rS.{tt}.{oh}"], writes=[f"rS.{tt}.{oh}"])
            tap("S2" + sfx, rS[:], [f"rS.{tt}.{ob}" for tt in range(NT) for ob in range(2)])

            P.barrier()
            for tt in range(NT):
                b = tt % 2
                rkeys = [f"rS.{tt}.0", f"rS.{tt}.1"]
                rap = rS[:, 1024 * tt:1024 * tt + 1024]
                ln_pass1(rap, rkeys, sq3, "sq3", tt, "F")
            ln_finish(NT, "F")
            for tt in range(NT):
                b = tt % 2
                rkeys = [f"rS.{tt}.0", f"rS.{tt}.1"]
                rap = rS[:, 1024 * tt:1024 * tt + 1024]
                mcol, rcol = ln_mr(tt)
                P.op("dve", lambda v, rap=rap, mcol=mcol, rcol=rcol: v.tensor_scalar(
                    rap, rap, mcol, rcol, ALU.subtract, ALU.mult),
                    reads=rkeys + ["F.m", "F.r"], writes=rkeys)
                P.op("dve", lambda v, rap=rap, b=b: v.tensor_copy(x1tmp[b][:], rap),
                     reads=rkeys, writes=[f"x1t{b}"])
                for j in range(8):
                    P.op("pe", lambda t, b=b, j=j: t.transpose(
                        pbT[:, 128 * j:128 * j + 128],
                        x1tmp[b][:, 128 * j:128 * j + 128], ident[:]),
                        reads=[f"x1t{b}", "ident"], writes=["pbT"])
                P.op("dve", lambda v, tt=tt: v.tensor_copy(
                    x1T[:].rearrange("p (j t) -> p j t", j=8)[:, :, 128 * tt:128 * tt + 128],
                    pbT[:].rearrange("p (j t) -> p j t", j=8)),
                    reads=["pbT"], writes=[f"x1T.{tt}"])
            tap("x1" + sfx, rS[:], [f"rS.{tt}.{ob}" for tt in range(NT) for ob in range(2)])

            for tt in range(NT):
                for kk in range(8):
                    P.op("pe", lambda t, tt=tt, kk=kk: t.matmul(
                        psb[6][:, 0:8],
                        x1T[:, S * kk + 128 * tt:S * kk + 128 * tt + 128],
                        c_rt[:, 8 * kk:8 * kk + 8],
                        start=(kk == 0), stop=(kk == 7)),
                        reads=[f"x1T.{tt}", "c_rt"], writes=["pb6"])
                P.op("dve", lambda v: v.tensor_copy(lgt[:], psb[6][:, 0:8]),
                     reads=["pb6"], writes=["lgt"])
                P.op("dve", lambda v: v.tensor_reduce(
                    stats[:, 8:9], lgt[:], AX.X, ALU.max),
                    reads=["lgt"], writes=["g.m1"])
                P.op("dve", lambda v: v.tensor_scalar(
                    eq1[:], lgt[:], stats[:, 8:9], None, ALU.is_equal),
                    reads=["lgt", "g.m1"], writes=["g.eq1"])
                P.op("dve", lambda v: v.scalar_tensor_tensor(
                    lmsk[:], eq1[:], -1e30, lgt[:], ALU.mult, ALU.add),
                    reads=["g.eq1", "lgt"], writes=["g.lm"])
                P.op("dve", lambda v: v.tensor_reduce(
                    stats[:, 9:10], lmsk[:], AX.X, ALU.max),
                    reads=["g.lm"], writes=["g.m2"])
                P.op("dve", lambda v: v.tensor_sub(
                    stats[:, 10:11], stats[:, 8:9], stats[:, 9:10]),
                    reads=["g.m1", "g.m2"], writes=["g.d"])
                P.op("act", lambda sc: sc.activation(
                    stats[:, 11:12], stats[:, 10:11], AF.Sigmoid),
                    reads=["g.d"], writes=["g.s1"])
                P.op("dve", lambda v: v.tensor_scalar(
                    stats[:, 12:13], stats[:, 11:12], -1.0, 1.0, ALU.mult, ALU.add),
                    reads=["g.s1"], writes=["g.w2"])
                P.op("dve", lambda v: v.tensor_scalar(
                    eq2[:], lgt[:], stats[:, 9:10], None, ALU.is_equal),
                    reads=["lgt", "g.m2"], writes=["g.eq2"])
                if moe_sparse:
                    P.op("dve", lambda v: v.tensor_add(maskt[:], eq1[:], eq2[:]),
                         reads=["g.eq1", "g.eq2"], writes=["g.mask"])
                P.op("dve", lambda v: v.tensor_scalar_mul(
                    t1t[:], eq1[:], stats[:, 11:12]),
                    reads=["g.eq1", "g.s1"], writes=["g.t1"])
                P.op("dve", lambda v: v.tensor_scalar_mul(
                    eq2[:], eq2[:], stats[:, 12:13]),
                    reads=["g.eq2", "g.w2"], writes=["g.eq2"])
                P.op("dve", lambda v, tt=tt: v.tensor_add(
                    c_sb[:, 8 * tt:8 * tt + 8], t1t[:], eq2[:]),
                    reads=["g.t1", "g.eq2"], writes=[f"comb.{tt}"])
                if moe_sparse:
                    if tt == 0:
                        P.op("dve", lambda v: v.memset(baserow[0:1, 0:8], -1.0),
                             writes=["gbase"])
                    P.op("dve", lambda v: v.tensor_copy(maskb[:], maskt[:]),
                         reads=["g.mask"], writes=["g.maskb"])
                    P.op("pe", lambda t: t.matmul(
                        psb[3][:, 0:8], c_onesf[0:1, 0:128], baserow[0:1, 0:8],
                        start=True, stop=True),
                        reads=["c_onesf", "gbase"], writes=["pb3"])
                    P.op("pe", lambda t: t.matmul(
                        psb[5][:, 0:8], c_tri[:], maskb[:],
                        start=True, stop=True),
                        reads=["c_tri", "g.maskb"], writes=["pb5"])
                    P.op("pe", lambda t: t.matmul(
                        psb[4][0:1, 0:8], ones[:, 0:1], maskb[:],
                        start=True, stop=True),
                        reads=["ones", "g.maskb"], writes=["pb4"])
                    P.op("dve", lambda v: v.scalar_tensor_tensor(
                        slotf[:], maskt[:], -65536.0, psb[5][:, 0:8],
                        ALU.mult, ALU.add),
                        reads=["g.mask", "pb5"], writes=["g.slotf"])
                    P.op("dve", lambda v: v.tensor_tensor(
                        slotf[:], slotf[:], psb[3][:, 0:8], ALU.add),
                        reads=["g.slotf", "pb3"], writes=["g.slotf"])
                    P.op("dve", lambda v: v.tensor_scalar(
                        slotf[:], slotf[:], 1.0, 65536.0, ALU.mult, ALU.add),
                        reads=["g.slotf"], writes=["g.slotf"])
                    P.op("dve", lambda v: v.tensor_scalar(
                        okt[:], slotf[:], float(CAP), None, ALU.is_lt),
                        reads=["g.slotf"], writes=["g.ok"])
                    P.op("dve", lambda v, tt=tt: v.tensor_tensor(
                        c_sb[:, 8 * tt:8 * tt + 8], c_sb[:, 8 * tt:8 * tt + 8],
                        okt[:], ALU.mult),
                        reads=[f"comb.{tt}", "g.ok"], writes=[f"comb.{tt}"])
                    P.op("dve", lambda v, tt=tt: v.tensor_copy(
                        sloti_sb[:, 8 * tt:8 * tt + 8], slotf[:]),
                        reads=["g.slotf"], writes=[f"sloti.{tt}"])
                    P.op("dve", lambda v: v.tensor_add(
                        baserow[0:1, 0:8], baserow[0:1, 0:8], psb[4][0:1, 0:8]),
                        reads=["gbase", "pb4"], writes=["gbase"])
                    bx2 = tt % 2
                    P.op("dve", lambda v, tt=tt, bx2=bx2: v.tensor_copy(
                        xsrc[bx2][:], rS[:, 1024 * tt:1024 * tt + 1024]),
                        reads=[f"rS.{tt}.0", f"rS.{tt}.1"], writes=[f"xsrc{bx2}"])
                    for e2 in range(E_EXP):
                        P.op("pool", lambda g, tt=tt, e2=e2, bx2=bx2:
                             g.indirect_dma_start(
                                 xg_hbm.rearrange("e c h -> (e c) h"),
                                 IndirectOffsetOnAxis(
                                     sloti_sb[:, 8 * tt + e2:8 * tt + e2 + 1], 0),
                                 xsrc[bx2][:], None,
                                 element_offset=e2 * CAP * HID,
                                 bounds_check=bc_reg(g), oob_is_err=False),
                             reads=[f"xsrc{bx2}", f"sloti.{tt}"],
                             writes=[f"xgh.{e2}"], dma=True, accum_write=True)
            tap("combine" + sfx, c_sb[:], [f"comb.{tt}" for tt in range(NT)])

            P.barrier()
            if moe_sparse:
                def emit_rb(e):
                    for st in range(NST):
                        bb = st % 2
                        P.op("pool", lambda g, e=e, st=st, bb=bb: g.dma_start(
                            out=xgt[bb][:],
                            in_=xg_hbm[e, 128 * st:128 * st + 128, :]),
                            reads=[f"xgh.{e}"], writes=[f"xgt{bb}"], dma=True)
                        for j in range(8):
                            P.op("pe", lambda t, bb=bb, j=j: t.transpose(
                                pbT[:, 128 * j:128 * j + 128],
                                xgt[bb][:, 128 * j:128 * j + 128], ident[:]),
                                reads=[f"xgt{bb}", "ident"], writes=["pbT"])
                        P.op("dve", lambda v, st=st: v.tensor_copy(
                            xgT[:].rearrange("p (j t) -> p j t", j=8)[:, :, 128 * st:128 * st + 128],
                            pbT[:].rearrange("p (j t) -> p j t", j=8)),
                            reads=["pbT"], writes=[f"xgT.{st}"])

                def emit_addback(e_src, tt):
                    bg = tt % 2
                    P.op("pool", lambda g, e=e_src, tt=tt, bg=bg: g.indirect_dma_start(
                        ygbuf[bg][:], None,
                        yb_hbm.rearrange("e c h -> (e c) h"),
                        IndirectOffsetOnAxis(
                            sloti_sb[:, 8 * tt + e:8 * tt + e + 1], 0),
                        element_offset=e * CAP * HID,
                        bounds_check=bc_reg(g), oob_is_err=False),
                        reads=[f"yb.{e_src}.{st}" for st in range(NST)]
                        + [f"sloti.{tt}"], writes=[f"ygb{bg}"], dma=True)
                    P.op("dve", lambda v, tt=tt, e=e_src, bg=bg: v.scalar_tensor_tensor(
                        rS[:, 1024 * tt:1024 * tt + 1024],
                        ygbuf[bg][:], c_sb[:, 8 * tt + e:8 * tt + e + 1],
                        rS[:, 1024 * tt:1024 * tt + 1024],
                        ALU.mult, ALU.add),
                        reads=[f"ygb{bg}", f"comb.{tt}", f"rS.{tt}.0", f"rS.{tt}.1"],
                        writes=[f"rS.{tt}.0", f"rS.{tt}.1"])

                pending = []
                emit_rb(0)
                for e in range(E_EXP):
                    for half in range(2):
                        i_base = half * (NIH * 128)
                        for ii in range(NIH):
                            b = ii % NWB
                            if pending:
                                emit_addback(*pending.pop(0))
                            prefetched = (e > 0 and half == 0 and ii < 2)
                            if ii == 2:
                                for ii2 in range(NIH):
                                    P.op("pool", lambda g, e=e, ii2=ii2, i_base=i_base: g.dma_start(
                                        out=wd_sb[:, 1024 * ii2:1024 * ii2 + 1024],
                                        in_=wd_t[e, i_base + 128 * ii2:i_base + 128 * ii2 + 128, :]),
                                        writes=[f"wd.{ii2}"], dma=True)
                            if not prefetched:
                                P.op("pool", lambda g, e=e, ii=ii, b=b, i_base=i_base: g.dma_start(
                                    out=wgbuf[b][:].rearrange("p (kk i) -> p kk i", kk=8),
                                    in_=wg_t.rearrange("e (kk p) i -> e p kk i", p=128)[e][:, :, i_base + 128 * ii:i_base + 128 * ii + 128]),
                                    writes=[f"wgb{b}"], dma=True)
                                P.op("pool", lambda g, e=e, ii=ii, b=b, i_base=i_base: g.dma_start(
                                    out=wubuf[b][:].rearrange("p (kk i) -> p kk i", kk=8),
                                    in_=wu_t.rearrange("e (kk p) i -> e p kk i", p=128)[e][:, :, i_base + 128 * ii:i_base + 128 * ii + 128]),
                                    writes=[f"wub{b}"], dma=True)
                            for tb in range(NTBC):
                                pg = tb % 2
                                pu = 2 + tb % 2
                                sb_ = tb % 2
                                xkeys = [f"xgT.{(TBW * tb) // 128 + i}"
                                         for i in range(TBW // 128)]
                                for kk in range(8):
                                    P.op("pe", lambda t, kk=kk, tb=tb, b=b, pg=pg:
                                         t.matmul(psb[pg][:, 0:TBW],
                                                  wgbuf[b][:, 128 * kk:128 * kk + 128],
                                                  xgT[:, CAP * kk + TBW * tb:CAP * kk + TBW * tb + TBW],
                                                  start=(kk == 0), stop=(kk == 7)),
                                         reads=[f"wgb{b}"] + xkeys, writes=[f"pb{pg}"])
                                P.op("act", lambda sc, pg=pg, sb_=sb_: sc.activation(
                                    silu_t[sb_][:, 0:TBW], psb[pg][:, 0:TBW], AF.Silu),
                                    reads=[f"pb{pg}"], writes=[f"silu{sb_}"])
                                for kk in range(8):
                                    P.op("pe", lambda t, kk=kk, tb=tb, b=b, pu=pu:
                                         t.matmul(psb[pu][:, 0:TBW],
                                                  wubuf[b][:, 128 * kk:128 * kk + 128],
                                                  xgT[:, CAP * kk + TBW * tb:CAP * kk + TBW * tb + TBW],
                                                  start=(kk == 0), stop=(kk == 7)),
                                         reads=[f"wub{b}"] + xkeys, writes=[f"pb{pu}"])
                                P.op("dve", lambda v, ii=ii, tb=tb, pu=pu, sb_=sb_: v.tensor_tensor(
                                    h_half[:, CAP * ii + TBW * tb:CAP * ii + TBW * tb + TBW],
                                    silu_t[sb_][:, 0:TBW], psb[pu][:, 0:TBW], ALU.mult),
                                    reads=[f"silu{sb_}", f"pb{pu}"], writes=[f"hh.{ii}"])
                        for st in range(NST):
                            for oh in range(2):
                                pk = 4 + (2 * st + oh) % 3
                                for kk in range(NIH):
                                    P.op("pe", lambda t, kk=kk, st=st, oh=oh, pk=pk:
                                         t.matmul(psb[pk][:],
                                                  h_half[:, CAP * kk + 128 * st:CAP * kk + 128 * st + 128],
                                                  wd_sb[:, 1024 * kk + 512 * oh:1024 * kk + 512 * oh + 512],
                                                  start=(kk == 0), stop=(kk == NIH - 1)),
                                         reads=[f"hh.{kk}", f"wd.{kk}"], writes=[f"pb{pk}"])
                                if half == 0:
                                    P.op("act", lambda sc, st=st, oh=oh, pk=pk: sc.activation(
                                        ysum[:, 1024 * st + 512 * oh:1024 * st + 512 * oh + 512],
                                        psb[pk][:], AF.Copy),
                                        reads=[f"pb{pk}"], writes=[f"ys.{st}.{oh}"])
                                else:
                                    P.op("dve", lambda v, st=st, oh=oh, pk=pk: v.tensor_tensor(
                                        ysum[:, 1024 * st + 512 * oh:1024 * st + 512 * oh + 512],
                                        ysum[:, 1024 * st + 512 * oh:1024 * st + 512 * oh + 512],
                                        psb[pk][:], ALU.add),
                                        reads=[f"pb{pk}", f"ys.{st}.{oh}"],
                                        writes=[f"ys.{st}.{oh}"])
                            if half == 1:
                                P.op("act", lambda sc, e=e, st=st: sc.dma_start(
                                    out=yb_hbm[e, 128 * st:128 * st + 128, :],
                                    in_=ysum[:, 1024 * st:1024 * st + 1024]),
                                    reads=[f"ys.{st}.0", f"ys.{st}.1"],
                                    writes=[f"yb.{e}.{st}"], dma=True)
                    if e + 1 < E_EXP:
                        emit_rb(e + 1)
                        for ii in range(2):
                            P.op("pool", lambda g, e1=e + 1, ii=ii: g.dma_start(
                                out=wgbuf[ii][:].rearrange("p (kk i) -> p kk i", kk=8),
                                in_=wg_t.rearrange("e (kk p) i -> e p kk i", p=128)[e1][:, :, 128 * ii:128 * ii + 128]),
                                writes=[f"wgb{ii}"], dma=True)
                            P.op("pool", lambda g, e1=e + 1, ii=ii: g.dma_start(
                                out=wubuf[ii][:].rearrange("p (kk i) -> p kk i", kk=8),
                                in_=wu_t.rearrange("e (kk p) i -> e p kk i", p=128)[e1][:, :, 128 * ii:128 * ii + 128]),
                                writes=[f"wub{ii}"], dma=True)
                    pending.extend((e, tt) for tt in range(NT))
                for es, tt in pending:
                    emit_addback(es, tt)
                pending = []
            if not moe_sparse:
              for e in range(E_EXP):
                for half in range(2):
                      i_base = half * (NIH * 128)
                      for ii in range(NIH):
                          b = ii % 4
                          if ii == 2:
                              for ii2 in range(NIH):
                                  P.op("pool", lambda g, e=e, ii2=ii2, i_base=i_base: g.dma_start(
                                      out=wd_sb[:, 1024 * ii2:1024 * ii2 + 1024],
                                      in_=wd_t[e, i_base + 128 * ii2:i_base + 128 * ii2 + 128, :]),
                                      writes=[f"wd.{ii2}"], dma=True)
                          P.op("pool", lambda g, e=e, ii=ii, b=b, i_base=i_base: g.dma_start(
                              out=wgbuf[b][:].rearrange("p (kk i) -> p kk i", kk=8),
                              in_=wg_t.rearrange("e (kk p) i -> e p kk i", p=128)[e][:, :, i_base + 128 * ii:i_base + 128 * ii + 128]),
                              writes=[f"wgb{b}"], dma=True)
                          P.op("pool", lambda g, e=e, ii=ii, b=b, i_base=i_base: g.dma_start(
                              out=wubuf[b][:].rearrange("p (kk i) -> p kk i", kk=8),
                              in_=wu_t.rearrange("e (kk p) i -> e p kk i", p=128)[e][:, :, i_base + 128 * ii:i_base + 128 * ii + 128]),
                              writes=[f"wub{b}"], dma=True)
                          for tb in range(NTB):
                              pg = tb % 2
                              pu = 2 + tb % 2
                              xkeys = [f"x1T.{4 * tb + i}" for i in range(4)]
                              for kk in range(8):
                                  P.op("pe", lambda t, kk=kk, tb=tb, b=b, pg=pg:
                                       t.matmul(psb[pg][:],
                                                wgbuf[b][:, 128 * kk:128 * kk + 128],
                                                x1T[:, S * kk + 512 * tb:S * kk + 512 * tb + 512],
                                                start=(kk == 0), stop=(kk == 7)),
                                       reads=[f"wgb{b}"] + xkeys, writes=[f"pb{pg}"])
                              sb_ = tb % 2
                              P.op("act", lambda sc, pg=pg, sb_=sb_: sc.activation(
                                  silu_t[sb_][:], psb[pg][:], AF.Silu),
                                  reads=[f"pb{pg}"], writes=[f"silu{sb_}"])
                              for kk in range(8):
                                  P.op("pe", lambda t, kk=kk, tb=tb, b=b, pu=pu:
                                       t.matmul(psb[pu][:],
                                                wubuf[b][:, 128 * kk:128 * kk + 128],
                                                x1T[:, S * kk + 512 * tb:S * kk + 512 * tb + 512],
                                                start=(kk == 0), stop=(kk == 7)),
                                       reads=[f"wub{b}"] + xkeys, writes=[f"pb{pu}"])
                              P.op("dve", lambda v, ii=ii, tb=tb, pu=pu, sb_=sb_: v.tensor_tensor(
                                  h_half[:, S * ii + 512 * tb:S * ii + 512 * tb + 512],
                                  silu_t[sb_][:], psb[pu][:], ALU.mult),
                                  reads=[f"silu{sb_}", f"pb{pu}"], writes=[f"hh.{ii}"])
                      for tt in range(NT):
                          for oh in range(2):
                              pk = 4 + (2 * tt + oh) % 3
                              for kk in range(NIH):
                                  P.op("pe", lambda t, kk=kk, tt=tt, oh=oh, pk=pk:
                                       t.matmul(psb[pk][:],
                                                h_half[:, S * kk + 128 * tt:S * kk + 128 * tt + 128],
                                                wd_sb[:, 1024 * kk + 512 * oh:1024 * kk + 512 * oh + 512],
                                                start=(kk == 0), stop=(kk == NIH - 1)),
                                       reads=[f"hh.{kk}", f"wd.{kk}"], writes=[f"pb{pk}"])
                              P.op("dve", lambda v, tt=tt, oh=oh, pk=pk, e=e:
                                   v.scalar_tensor_tensor(
                                       rS[:, 1024 * tt + 512 * oh:1024 * tt + 512 * oh + 512],
                                       psb[pk][:], c_sb[:, 8 * tt + e:8 * tt + e + 1],
                                       rS[:, 1024 * tt + 512 * oh:1024 * tt + 512 * oh + 512],
                                       ALU.mult, ALU.add),
                                   reads=[f"pb{pk}", f"comb.{tt}", f"rS.{tt}.{oh}"],
                                   writes=[f"rS.{tt}.{oh}"])
            tap("premoe_out" + sfx, rS[:],
                [f"rS.{tt}.{ob}" for tt in range(NT) for ob in range(2)])

            P.barrier()
            for tt in range(NT):
                rkeys = [f"rS.{tt}.0", f"rS.{tt}.1"]
                rap = rS[:, 1024 * tt:1024 * tt + 1024]
                ln_pass1(rap, rkeys, sq4, "sq4", tt, "I")
            ln_finish(NT, "I")
            for tt in range(NT):
                b = tt % 2
                rkeys = [f"rS.{tt}.0", f"rS.{tt}.1"]
                rap = rS[:, 1024 * tt:1024 * tt + 1024]
                mcol, rcol = ln_mr(tt)
                P.op("dve", lambda v, rap=rap, b=b, mcol=mcol, rcol=rcol: v.tensor_scalar(
                    outb[b][:], rap, mcol, rcol,
                    ALU.subtract, ALU.mult),
                    reads=rkeys + ["I.m", "I.r"], writes=[f"outb{b}"])
                P.op("sp", lambda sp, tt=tt, b=b, s=s: sp.dma_start(
                    out=y[s, 128 * tt:128 * tt + 128, :], in_=outb[b][:]),
                    reads=[f"outb{b}"], dma=True)
            P.barrier()

        P.op("sp", lambda sp: sp.nop(),
             writes=[f"outb{b}" for b in range(2)])

        block = ctx.enter_context(nc.Block())
        P.emit(block, sems)
    return nc


def _prep_inputs(kw, shard_x):
    bf = ml_dtypes.bfloat16
    rep = lambda a: np.ascontiguousarray(
        np.broadcast_to(np.asarray(a, np.float32), (128, a.shape[-1])))
    S = shard_x.shape[1]
    m = {
        "x": np.ascontiguousarray(shard_x, dtype=np.float32),
        "wq": np.ascontiguousarray(kw["L0_attn_wq"], np.float32),
        "wk": np.ascontiguousarray(kw["L0_attn_wk"], np.float32),
        "wv": np.ascontiguousarray(kw["L0_attn_wv"], np.float32),
        "wproj": np.ascontiguousarray(kw["L0_attn_wproj"], np.float32),
        "w1": np.ascontiguousarray(kw["L0_attn_w1"], np.float32),
        "w2": np.ascontiguousarray(kw["L0_attn_w2"], np.float32),
        "router_t": np.ascontiguousarray(kw["L0_ffn_router_w"].T, np.float32),
        "wg_t": np.ascontiguousarray(np.asarray(kw["L0_ffn_Wg"], np.float32).transpose(0, 2, 1)),
        "wu_t": np.ascontiguousarray(np.asarray(kw["L0_ffn_Wu"], np.float32).transpose(0, 2, 1)),
        "wd_t": np.ascontiguousarray(np.asarray(kw["L0_ffn_Wd"], np.float32).transpose(0, 2, 1)),
        "c_ident": np.eye(128, dtype=np.float32).astype(bf),
        "c_ones": np.ones((128, 128), dtype=np.float32).astype(bf),
        "c_alpha": alpha_table(S),
        "c_ln1g": rep(kw["L0_attn_ln1_g"]),
        "c_ln1b": rep(kw["L0_attn_ln1_b"]),
        "c_ln2g": rep(kw["L0_attn_ln2_g"]),
        "c_ln2b": rep(kw["L0_attn_ln2_b"]),
        "c_bv": rep(kw["L0_attn_bv"]),
        "c_bproj": rep(kw["L0_attn_bproj"]),
        "c_b2": rep(kw["L0_attn_b2"]),
        "c_bq": np.ascontiguousarray(
            np.asarray(kw["L0_attn_bq"], np.float32).reshape(8, 128).T),
        "c_bk": np.ascontiguousarray(
            np.asarray(kw["L0_attn_bk"], np.float32).reshape(8, 128).T),
        "c_b1": np.ascontiguousarray(
            np.asarray(kw["L0_attn_b1"], np.float32).reshape(32, 128).T),
        "c_tri": np.ascontiguousarray(np.triu(np.ones((128, 128), np.float32)).astype(ml_dtypes.bfloat16)),
        "c_onesf": np.ones((128, 128), np.float32),
    }
    return m


def kernel(**kw):
    x_full = np.asarray(kw["L0_attn_hidden_states"], np.float32)
    B, S, _ = x_full.shape
    n_cores = 8
    per = B // n_cores
    nc = build_program(S=S, NSEQ=per, moe_sparse=True, CAP=S // 2)
    in_maps = [_prep_inputs(kw, x_full[c * per:(c + 1) * per]) for c in range(n_cores)]
    res = run_bass_kernel_spmd(nc, in_maps, core_ids=list(range(n_cores)))
    return np.concatenate([np.asarray(r["y"]) for r in res.results], axis=0)


# version 13
# speedup vs baseline: 1.1575x; 1.1575x over previous
import bisect
import numpy as np
import ml_dtypes
from contextlib import ExitStack

import concourse.bass as bass
import concourse.mybir as mybir
from concourse.bass_utils import run_bass_kernel_spmd
from concourse.bass import IndirectOffsetOnAxis
from concourse.alu_op_type import AluOpType as ALU

F32 = mybir.dt.float32
BF16 = mybir.dt.bfloat16
U8 = mybir.dt.uint8
I32 = mybir.dt.int32
AF = mybir.ActivationFunctionType
AX = mybir.AxisListType

ENGINES = ("pe", "act", "dve", "pool", "sp")
ENG_ATTR = {"pe": "tensor", "act": "scalar", "dve": "vector", "pool": "gpsimd", "sp": "sync"}
DMA_Q = {"sp": "qsp", "act": "qact", "pool": "qpool"}

HID = 1024
NH = 16
HD = 64
EPS = 1e-5
E_EXP = 8
I_MOE = 2816
I_FFN = 4096


DMA_DEPTH = 8


class _Op:
    __slots__ = ("engine", "fn", "eng_idx", "waits", "drain", "inc", "dma_sem", "dma_v")

    def __init__(self, engine, fn, eng_idx):
        self.engine = engine
        self.fn = fn
        self.eng_idx = eng_idx
        self.waits = []
        self.drain = False
        self.inc = False
        self.dma_sem = None
        self.dma_v = 0


class Prog:

    def __init__(self):
        self.eng_ops = {e: [] for e in ENGINES}
        self.writers = {}
        self.readers = {}
        self.att = {e: ([], []) for e in ENGINES}
        self.dma_n = {"qsp": 0, "qact": 0, "qpool": 0}
        self.dma_sub = {(q, i): 0 for q in ("qsp", "qact", "qpool")
                        for i in range(DMA_DEPTH)}
        self.drain_after = {e: 0 for e in ENGINES}
        self.waited = {e: {} for e in ENGINES}
        self.fence = {e: [] for e in ENGINES}
        self.last_op = {e: None for e in ENGINES}
        self.last_nondma = {e: None for e in ENGINES}
        self.n_drains = 0
        self.n_waits = 0

    def _milestone(self, op):
        e = op.engine
        idxs, vals = self.att[e]
        i = bisect.bisect_left(idxs, op.eng_idx)
        if i < len(idxs):
            return ("eng", e), vals[i]
        op.inc = True
        v = (vals[-1] if vals else 0) + 1
        idxs.append(op.eng_idx)
        vals.append(v)
        return ("eng", e), v

    def _want(self, cur, semk, v):
        if self.waited[cur.engine].get(semk, 0) >= v:
            return
        cur.waits.append((semk, v))
        self.waited[cur.engine][semk] = v
        self.n_waits += 1

    def _dep(self, cur, prod):
        if isinstance(prod, tuple):
            semk, v = prod
            self._want(cur, semk, v)
        elif prod.engine == cur.engine:
            if cur.engine == "pe":
                return
            if prod.eng_idx >= self.drain_after[cur.engine]:
                if not cur.drain:
                    cur.drain = True
                    self.n_drains += 1
                self.drain_after[cur.engine] = cur.eng_idx
        else:
            semk, v = self._milestone(prod)
            self._want(cur, semk, v)

    def op(self, engine, fn, reads=(), writes=(), dma=False, accum_write=False):
        cur = _Op(engine, fn, len(self.eng_ops[engine]))
        for semk, v in self.fence[engine]:
            self._want(cur, semk, v)
        self.fence[engine] = []
        for k in reads:
            w = self.writers.get(k)
            if w:
                for op2 in w["eng"].values():
                    self._dep(cur, op2)
                for semk, v in w["dma"].items():
                    self._dep(cur, (semk, v))
        for k in writes:
            w = self.writers.get(k)
            if w:
                for op2 in w["eng"].values():
                    self._dep(cur, op2)
                for semk, v in w["dma"].items():
                    self._dep(cur, (semk, v))
            r = self.readers.get(k)
            if r:
                for op2 in r["eng"].values():
                    self._dep(cur, op2)
                for semk, v in r["dma"].items():
                    self._dep(cur, (semk, v))
        if dma:
            q = DMA_Q[engine]
            sub = self.dma_n[q] % DMA_DEPTH
            self.dma_n[q] += 1
            self.dma_sub[(q, sub)] += 1
            cnt = self.dma_sub[(q, sub)]
            cur.dma_sem = ("dma", q, sub)
            cur.dma_v = 16 * cnt
            if cnt > 1:
                self._want(cur, cur.dma_sem, 16 * (cnt - 1))
        for k in reads:
            rd = self.readers.setdefault(k, {"eng": {}, "dma": {}})
            if dma:
                rd["dma"][cur.dma_sem] = cur.dma_v
            else:
                rd["eng"][engine] = cur
        for k in writes:
            if accum_write:
                w = self.writers.setdefault(k, {"eng": {}, "dma": {}})
                if dma:
                    w["dma"][cur.dma_sem] = cur.dma_v
                else:
                    w["eng"][engine] = cur
            elif dma:
                self.writers[k] = {"eng": {}, "dma": {cur.dma_sem: cur.dma_v}}
                self.readers[k] = {"eng": {}, "dma": {}}
            else:
                self.writers[k] = {"eng": {engine: cur}, "dma": {}}
                self.readers[k] = {"eng": {}, "dma": {}}
        self.eng_ops[engine].append(cur)
        self.last_op[engine] = cur
        if not dma:
            self.last_nondma[engine] = cur
        return cur

    def barrier(self):
        pend = []
        for e in ENGINES:
            if self.last_nondma[e] is not None:
                semk, v = self._milestone(self.last_nondma[e])
                pend.append((semk, v))
        for (q, sub), n in self.dma_sub.items():
            if n:
                pend.append((("dma", q, sub), 16 * n))
        for e in ENGINES:
            self.fence[e] = list(pend)

    def emit(self, block, sems):
        for e in ENGINES:
            ops = self.eng_ops[e]

            def body(eng, ops=ops, e=e):
                nc = eng.bass
                with nc.allow_low_precision(reason="bf16 operands per problem tolerance"):
                    for op in ops:
                        for semk, v in op.waits:
                            eng.wait_ge(sems[semk], v)
                        if op.drain:
                            eng.drain()
                        instr = op.fn(eng)
                        if op.dma_sem:
                            instr.then_inc(sems[op.dma_sem], 16)
                        elif op.inc:
                            instr.then_inc(sems[("eng", e)], 1)

            getattr(block, ENG_ATTR[e])(body)


class Arena:

    def __init__(self, nc, ctx, total_bytes):
        self.nc = nc
        self.handle = ctx.enter_context(
            nc.sbuf_tensor("arena", [128, total_bytes], U8))
        self.base = nc.lookup_mloc(self.handle).addr
        self.total = total_bytes

    def place(self, name, shape, dtype, offset):
        per_part = int(np.prod(shape[1:])) * mybir.dt.size(dtype)
        assert offset + per_part <= self.total, \
            f"{name}: {offset}+{per_part} > {self.total}"
        assert offset % 32 == 0, name
        return self.nc.alloc_sbuf_tensor_at(name, list(shape), dtype,
                                            offset=self.base + offset)


def windows_for(S):
    out = []
    for i in range(0, S, 256):
        out.append((max(0, i - 128), min(S, i + 384)))
    return out


def alpha_table(S):
    wins = windows_for(S)
    nw = len(wins)
    al = np.arange(256, dtype=np.float64) / 255.0
    row = np.zeros(512 * nw, dtype=np.float64)
    for w, (start, end) in enumerate(wins):
        L = end - start
        ov = 0 if w == 0 else 256
        for c in range(L):
            if c < ov:
                val = al[c]
            elif w < nw - 1 and c >= L - 256:
                val = 1.0 - al[c - (L - 256)]
            else:
                val = 1.0
            row[512 * w + c] = val
    out = np.broadcast_to(row.astype(np.float32), (128, 512 * nw))
    return np.ascontiguousarray(out.astype(ml_dtypes.bfloat16))


def build_program(S=2048, NSEQ=2, taps=(), moe_sparse=True, CAP=1024):
    nc = bass.Bass("TRN2", target_bir_lowering=False)
    NT = S // 128
    NTB = S // 512
    wins = windows_for(S)
    NW = len(wins)
    NST = CAP // 128
    TBW = min(512, CAP)
    NTBC = CAP // TBW
    NIH = I_MOE // 256
    scale = float(HD) ** (-0.5)

    dr = {}
    def din(name, shape, dt):
        dr[name] = nc.dram_tensor(name, shape, dt, kind="ExternalInput").ap()
        return dr[name]

    x = din("x", [NSEQ, S, HID], F32)
    y = nc.dram_tensor("y", [NSEQ, S, HID], F32, kind="ExternalOutput").ap()
    wq = din("wq", [HID, HID], F32)
    wk = din("wk", [HID, HID], F32)
    wv = din("wv", [HID, HID], F32)
    wproj = din("wproj", [HID, HID], F32)
    w1 = din("w1", [HID, I_FFN], F32)
    w2 = din("w2", [I_FFN, HID], F32)
    router_t = din("router_t", [HID, E_EXP], F32)
    wg_t = din("wg_t", [E_EXP, HID, I_MOE], F32)
    wu_t = din("wu_t", [E_EXP, HID, I_MOE], F32)
    wd_t = din("wd_t", [E_EXP, I_MOE, HID], F32)
    d_ident = din("c_ident", [128, 128], BF16)
    d_ones = din("c_ones", [128, 128], BF16)
    d_alpha = din("c_alpha", [128, 512 * NW], BF16)
    d_ln1g = din("c_ln1g", [128, HID], F32)
    d_ln1b = din("c_ln1b", [128, HID], F32)
    d_ln2g = din("c_ln2g", [128, HID], F32)
    d_ln2b = din("c_ln2b", [128, HID], F32)
    d_bv = din("c_bv", [128, HID], F32)
    d_bproj = din("c_bproj", [128, HID], F32)
    d_b2 = din("c_b2", [128, HID], F32)
    d_bq = din("c_bq", [128, 8], F32)
    d_bk = din("c_bk", [128, 8], F32)
    d_b1 = din("c_b1", [128, 32], F32)
    d_tri = din("c_tri", [128, 128], BF16)
    d_onesf = din("c_onesf", [128, 128], F32)
    xg_hbm = nc.dram_tensor("xg_hbm", [E_EXP, CAP, HID], BF16, kind="Internal").ap()
    yb_hbm = nc.dram_tensor("yb_hbm", [E_EXP, CAP, HID], F32, kind="Internal").ap()

    tap_out = {}
    for tname, tshape, tdt in taps:
        tap_out[tname] = nc.dram_tensor("tap_" + tname, list(tshape), tdt,
                                        kind="ExternalOutput").ap()

    P = Prog()
    ctx = ExitStack()
    with ctx:
        KB = 1024
        ar = Arena(nc, ctx, 204 * KB)
        PB = 196 * KB
        ident = ar.place("ident", [128, 128], BF16, PB)
        ones = ar.place("ones", [128, 128], BF16, PB + 256)
        c_bq = ar.place("c_bq", [128, 8], F32, PB + 512)
        c_bk = ar.place("c_bk", [128, 8], F32, PB + 544)
        c_b1 = ar.place("c_b1", [128, 32], F32, PB + 576)
        c_rt = ar.place("c_rt", [128, 64], BF16, PB + 704)
        stats = ar.place("stats", [128, 16], F32, PB + 960)
        statb = ar.place("statb", [128, 80], F32, PB + 2208)
        c_sb = ar.place("c_sb", [128, NT * 8], F32, PB + 1024)
        lgt = ar.place("lgt", [128, 8], F32, PB + 2048)
        eq1 = ar.place("eq1", [128, 8], F32, PB + 2080)
        eq2 = ar.place("eq2", [128, 8], F32, PB + 2112)
        lmsk = ar.place("lmsk", [128, 8], F32, PB + 2144)
        t1t = ar.place("t1t", [128, 8], F32, PB + 2176)
        W0, W1, W3, W4, W5 = 0, 32 * KB, 96 * KB, 128 * KB, 160 * KB

        psb = [ctx.enter_context(nc.psum_tensor(f"pb{i}", [128, 512], F32))
               for i in range(7)]
        pbT = ctx.enter_context(nc.psum_tensor("pbT", [128, 1024], BF16))

        sems = {}
        for e in ENGINES:
            sems[("eng", e)] = ctx.enter_context(nc.semaphore(f"sem_{e}"))
        for q in ("qsp", "qact", "qpool"):
            for i in range(DMA_DEPTH):
                sems[("dma", q, i)] = ctx.enter_context(nc.semaphore(f"sem_{q}{i}"))

        _bcreg = {}

        def bc_reg(g):
            if "r" not in _bcreg:
                _bcreg["r"] = g.to_reg(CAP - 1)
            return _bcreg["r"]

        def ldconst(dst, src, key):
            P.op("sp", lambda s, d=dst, sr=src: s.dma_start(out=d[:], in_=sr),
                 writes=[key], dma=True)

        def tap(name, ap, keys):
            if name in tap_out:
                P.op("sp", lambda s, a=ap, o=tap_out[name]: s.dma_start(out=o, in_=a),
                     reads=list(keys), dma=True)

        def ln_pass1(tile_ap, tile_keys, sq_scr, sq_key, col, tag):
            P.op("dve", lambda v, a=tile_ap, col=col: v.tensor_reduce(
                statb[:, col:col + 1], a, AX.X, ALU.add),
                reads=tile_keys, writes=[f"{tag}.sx{col}"])
            P.op("act", lambda sc, a=tile_ap, col=col: sc.activation(
                sq_scr[:], a, AF.Square, accum_out=statb[:, 16 + col:17 + col]),
                reads=tile_keys, writes=[sq_key, f"{tag}.sq{col}"])

        def ln_finish(n, tag):
            sxk = [f"{tag}.sx{c}" for c in range(n)]
            sqk = [f"{tag}.sq{c}" for c in range(n)]
            P.op("dve", lambda v, n=n: v.tensor_scalar_mul(
                statb[:, 32:32 + n], statb[:, 0:n], 1.0 / HID),
                reads=sxk, writes=[f"{tag}.m"])
            P.op("dve", lambda v, n=n: v.tensor_scalar(
                statb[:, 48:48 + n], statb[:, 16:16 + n], 1.0 / HID, EPS,
                ALU.mult, ALU.add), reads=sqk, writes=[f"{tag}.v"])
            P.op("act", lambda sc, n=n: sc.activation(
                statb[:, 64:64 + n], statb[:, 32:32 + n], AF.Square),
                reads=[f"{tag}.m"], writes=[f"{tag}.mm"])
            P.op("dve", lambda v, n=n: v.tensor_sub(
                statb[:, 48:48 + n], statb[:, 48:48 + n], statb[:, 64:64 + n]),
                reads=[f"{tag}.v", f"{tag}.mm"], writes=[f"{tag}.v"])
            P.op("act", lambda sc, n=n: sc.activation(
                statb[:, 64:64 + n], statb[:, 48:48 + n], AF.Ln),
                reads=[f"{tag}.v"], writes=[f"{tag}.l"])
            P.op("act", lambda sc, n=n: sc.activation(
                statb[:, 48:48 + n], statb[:, 64:64 + n], AF.Exp, scale=-0.5),
                reads=[f"{tag}.l"], writes=[f"{tag}.r"])

        def ln_mr(col):
            return statb[:, 32 + col:33 + col], statb[:, 48 + col:49 + col]

        xnT = ar.place("xnT", [128, 8 * S], BF16, W0)
        qT = ar.place("qT", [128, 8 * S], BF16, W1)
        kT = ar.place("kT", [128, 8 * S], BF16, W1 + 32 * KB)
        v_tm = ar.place("v_tm", [128, NT * 1024], BF16, W3)
        x0t = [ar.place(f"x0t{b}", [128, HID], F32, W5 + b * 4 * KB) for b in range(4)]
        xn_tmp = [ar.place(f"xnt{b}", [128, HID], BF16, W5 + 16 * KB + b * 2 * KB)
                  for b in range(2)]
        sq_scr = ar.place("sq", [128, HID], F32, W5 + 20 * KB)
        c_ln1g = ar.place("ln1g", [128, HID], F32, W5 + 24 * KB)
        c_ln1b = ar.place("ln1b", [128, HID], F32, W5 + 28 * KB)
        c_bv = ar.place("bv", [128, HID], F32, W5 + 32 * KB)
        wbuf = [ar.place(f"wbuf{b}", [128, 8 * 1024], BF16, W4 + b * 16 * KB)
                for b in range(2)]
        zsrc_f32 = ar.place("zsrcf", [128, NST * HID], F32, W5)
        ctxT = ar.place("ctx", [128, 8 * S], BF16, W0)
        expE = ar.place("expE", [128, 2048], BF16, W4)
        expO = ar.place("expO", [128, 2048], BF16, W4 + 4 * KB)
        wrow = ar.place("wrow", [128, 512], BF16, W4 + 8 * KB)
        wb_sb = ar.place("wbsb", [128, 512], F32, W4 + 9 * KB)
        tmp256 = ar.place("tmp256", [128, 256], BF16, W4 + 11 * KB)
        c_alpha = ar.place("alpha", [128, 512 * NW], BF16, W4 + 12 * KB)
        lnr = ar.place("lnr", [128, 512], F32, W4 + 12 * KB + ((512 * NW * 2 + 1023) // 1024) * 1024)
        rS = ar.place("rS", [128, NT * 1024], F32, W1)
        o_bf = ar.place("obf", [128, NT * 1024], BF16, W3)
        wbufP = ar.place("wbufP", [128, 8 * 1024], BF16, W4)
        x0t2 = [ar.place(f"x0u{b}", [128, HID], F32, W5 + b * 4 * KB) for b in range(2)]
        c_bproj = ar.place("bproj", [128, HID], F32, W5 + 8 * KB)
        c_b2c = ar.place("b2c", [128, HID], F32, W5 + 12 * KB)
        c_pb2 = ar.place("pb2", [128, HID], F32, W5 + 16 * KB)
        xbA = [ar.place(f"xbA{b}", [128, HID], F32, W5 + 20 * KB + b * 4 * KB)
               for b in range(2)]
        xn2T = ar.place("xn2T", [128, 8 * 512], BF16, W0)
        w1buf = [ar.place(f"w1b{b}", [128, 8 * 512], BF16, W0 + 8 * KB + b * 8 * KB)
                 for b in range(2)]
        w2buf = [ar.place(f"w2b{b}", [128, 4 * 512], BF16, W0 + 24 * KB + b * 4 * KB)
                 for b in range(2)]
        h1T = ar.place("h1T", [128, 32 * 512], BF16, W4)
        xn2tm = ar.place("xn2tm", [128, HID], BF16, W5)
        sq2 = ar.place("sq2", [128, HID], F32, W5 + 4 * KB)
        c_ln2g = ar.place("ln2g", [128, HID], F32, W5 + 8 * KB)
        c_ln2b = ar.place("ln2b", [128, HID], F32, W5 + 12 * KB)
        x1T = ar.place("x1T", [128, 8 * S], BF16, W3)
        x1tmp = [ar.place(f"x1t{b}", [128, HID], BF16, W5 + b * 2 * KB) for b in range(2)]
        sq3 = ar.place("sq3", [128, HID], F32, W5 + 4 * KB)
        wd_sb = ar.place("wd", [128, NIH * 1024], BF16, W0)
        NWB = 4
        if moe_sparse:
            wgbuf = [ar.place(f"wgb{b}", [128, 8 * 128], BF16, W3 + 24 * KB + b * 2 * KB)
                     for b in range(NWB)]
            wubuf = [ar.place(f"wub{b}", [128, 8 * 128], BF16, W4 + 54 * KB + b * 2 * KB)
                     for b in range(NWB)]
            silu_t = [ar.place(f"silu{b}", [128, 512], F32, W4 + 62 * KB + b * 2 * KB)
                      for b in range(2)]
        else:
            HB = W4 + 44 * KB
            wgbuf = [ar.place(f"wgb{b}", [128, 8 * 128], BF16, HB + b * 2 * KB)
                     for b in range(NWB)]
            wubuf = [ar.place(f"wub{b}", [128, 8 * 128], BF16, HB + NWB * 2 * KB + b * 2 * KB)
                     for b in range(NWB)]
            silu_t = [ar.place(f"silu{b}", [128, 512], F32, HB + NWB * 4 * KB + b * 2 * KB)
                      for b in range(2)]
        h_half = ar.place("hh", [128, NIH * (CAP if moe_sparse else S)], BF16, W4)
        outb = [ar.place(f"outb{b}", [128, HID], F32, W0 + b * 4 * KB) for b in range(2)]
        sq4 = ar.place("sq4", [128, HID], F32, W0 + 8 * KB)
        sloti_sb = ar.place("sloti", [128, 8 * NT], I32, PB + 2560)
        maskt = ar.place("maskt", [128, 8], F32, PB + 3104)
        maskb = ar.place("maskb", [128, 8], BF16, PB + 3072)
        baserow = ar.place("baserow", [128, 8], F32, PB + 3136)
        slotf = ar.place("slotf", [128, 8], F32, PB + 3168)
        okt = ar.place("okt", [128, 8], F32, PB + 3200)
        c_tri = ar.place("c_tri", [128, 128], BF16, PB + 3232)
        c_onesf = ar.place("c_onesf", [128, 128], F32, PB + 3744)
        xsrc = [ar.place(f"xsrc{b}", [128, HID], BF16, W5 + 8 * KB + b * 2 * KB)
                for b in range(2)]
        xgT = ar.place("xgT", [128, 8 * CAP], BF16, W3)
        ygbuf = [ar.place(f"yg{b}", [128, HID], F32, W3 + 16 * KB + b * 4 * KB)
                 for b in range(2)]
        xgt = [ar.place(f"xgt{b}", [128, HID], BF16, W0 + 22 * KB + b * 2 * KB)
               for b in range(2)]
        ysum = ar.place("ysum", [128, NST * HID], F32, W4 + 22 * KB)

        for s in range(NSEQ):
            sfx = f".s{s}"


            if s == 0:
                ldconst(ident, d_ident, "ident")
                ldconst(ones, d_ones, "ones")
                ldconst(c_bq, d_bq, "c_bq")
                ldconst(c_bk, d_bk, "c_bk")
                ldconst(c_b1, d_b1, "c_b1")
                if moe_sparse:
                    ldconst(c_tri, d_tri, "c_tri")
                    ldconst(c_onesf, d_onesf, "c_onesf")
                P.op("pool", lambda g: g.dma_start(
                    out=c_rt[:].rearrange("p (kk e) -> p kk e", kk=8),
                    in_=router_t.rearrange("(kk p) e -> p kk e", p=128)),
                    writes=["c_rt"], dma=True)
            ldconst(c_ln1g, d_ln1g, "ln1g")
            ldconst(c_ln1b, d_ln1b, "ln1b")
            ldconst(c_bv, d_bv, "c_bv")

            for tt in range(NT):
                b = tt % 4
                P.op("sp", lambda sp, b=b, tt=tt, s=s: sp.dma_start(
                    out=x0t[b][:], in_=x[s, 128 * tt:128 * tt + 128, :]),
                    writes=[f"x0t{b}"], dma=True)
                ln_pass1(x0t[b][:], [f"x0t{b}"], sq_scr, "sq", tt, "A")
            ln_finish(NT, "A")
            for tt in range(NT):
                bx = tt % 4
                b = tt % 2
                P.op("sp", lambda sp, bx=bx, tt=tt, s=s: sp.dma_start(
                    out=x0t[bx][:], in_=x[s, 128 * tt:128 * tt + 128, :]),
                    writes=[f"x0t{bx}"], dma=True)
                mcol, rcol = ln_mr(tt)
                P.op("dve", lambda v, bx=bx, mcol=mcol, rcol=rcol: v.tensor_scalar(
                    sq_scr[:], x0t[bx][:], mcol, rcol,
                    ALU.subtract, ALU.mult),
                    reads=[f"x0t{bx}", "A.m", "A.r"], writes=["sq"])
                P.op("dve", lambda v: v.tensor_tensor(
                    sq_scr[:], sq_scr[:], c_ln1g[:], ALU.mult),
                    reads=["sq", "ln1g"], writes=["sq"])
                P.op("dve", lambda v, b=b: v.tensor_tensor(
                    xn_tmp[b][:], sq_scr[:], c_ln1b[:], ALU.add),
                    reads=["sq", "ln1b"], writes=[f"xnt{b}"])
                for j in range(8):
                    P.op("pe", lambda t, b=b, j=j: t.transpose(
                        pbT[:, 128 * j:128 * j + 128],
                        xn_tmp[b][:, 128 * j:128 * j + 128], ident[:]),
                        reads=[f"xnt{b}", "ident"], writes=["pbT"])
                P.op("dve", lambda v, tt=tt: v.tensor_copy(
                    xnT[:].rearrange("p (j t) -> p j t", j=8)[:, :, 128 * tt:128 * tt + 128],
                    pbT[:].rearrange("p (j t) -> p j t", j=8)),
                    reads=["pbT"], writes=[f"xnT.{tt}"])
            tap("xn" + sfx, xnT[:], [f"xnT.{tt}" for tt in range(NT)])


            def ld_w1024(dst, src, key):
                P.op("pool", lambda g, d=dst, sr=src: g.dma_start(
                    out=d[:].rearrange("p (kk o) -> p kk o", kk=8),
                    in_=sr.rearrange("(kk p) o -> p kk o", p=128)),
                    writes=[key], dma=True)

            ld_w1024(wbuf[0], wq, "wbuf0")
            ld_w1024(wbuf[1], wk, "wbuf1")

            pbi = [0]
            def nxpb(n=4):
                pbi[0] = (pbi[0] + 1) % n
                return pbi[0]

            def qkv_fm(dst, dstkey, wb, wkey, bias_t, bias_key):
                for j in range(8):
                    for tb in range(NTB):
                        pk = nxpb()
                        for kk in range(8):
                            P.op("pe", lambda t, j=j, tb=tb, kk=kk, pk=pk, wb=wb:
                                 t.matmul(psb[pk][:],
                                          wb[:, 1024 * kk + 128 * j:1024 * kk + 128 * j + 128],
                                          xnT[:, S * kk + 512 * tb:S * kk + 512 * tb + 512],
                                          start=(kk == 0), stop=(kk == 7)),
                                 reads=[wkey] + [f"xnT.{4 * tb + i}" for i in range(4)],
                                 writes=[f"pb{pk}"])
                        P.op("act", lambda sc, j=j, tb=tb, pk=pk, dst=dst, bias_t=bias_t:
                             sc.activation(dst[:, S * j + 512 * tb:S * j + 512 * tb + 512],
                                           psb[pk][:], AF.Identity, bias=bias_t[:, j:j + 1]),
                             reads=[f"pb{pk}", bias_key], writes=[f"{dstkey}.{j}.{tb}"])

            qkv_fm(qT, "qT", wbuf[0], "wbuf0", c_bq, "c_bq")
            ld_w1024(wbuf[0], wv, "wbuf0")
            qkv_fm(kT, "kT", wbuf[1], "wbuf1", c_bk, "c_bk")
            for tt in range(NT):
                for ob in range(2):
                    pk = nxpb()
                    for kk in range(8):
                        P.op("pe", lambda t, tt=tt, ob=ob, kk=kk, pk=pk:
                             t.matmul(psb[pk][:],
                                      xnT[:, S * kk + 128 * tt:S * kk + 128 * tt + 128],
                                      wbuf[0][:, 1024 * kk + 512 * ob:1024 * kk + 512 * ob + 512],
                                      start=(kk == 0), stop=(kk == 7)),
                             reads=["wbuf0", f"xnT.{tt}"], writes=[f"pb{pk}"])
                    P.op("dve", lambda v, tt=tt, ob=ob, pk=pk: v.tensor_tensor(
                        v_tm[:, 1024 * tt + 512 * ob:1024 * tt + 512 * ob + 512],
                        psb[pk][:], c_bv[:, 512 * ob:512 * ob + 512], ALU.add),
                        reads=[f"pb{pk}", "c_bv"], writes=[f"v.{tt}"])
            tap("q" + sfx, qT[:], [f"qT.{j}.{b}" for j in range(8) for b in range(NTB)])
            tap("k" + sfx, kT[:], [f"kT.{j}.{b}" for j in range(8) for b in range(NTB)])
            tap("v" + sfx, v_tm[:], [f"v.{tt}" for tt in range(NT)])

            P.barrier()
            if moe_sparse and s == 0:
                zf32 = zsrc_f32
                P.op("dve", lambda v, zf32=zf32: v.memset(zf32[:], 0.0),
                     writes=["zsrc"])
                for e0 in range(E_EXP):
                    P.op("act", lambda sc, e0=e0, zf32=zf32: sc.dma_start(
                        out=yb_hbm[e0].rearrange("(n p) h -> p n h", p=128),
                        in_=zf32[:].rearrange("p (n h) -> p n h", n=NST)),
                        reads=["zsrc"],
                        writes=[f"yb.{e0}.{st0}" for st0 in range(NST)],
                        dma=True)
                    P.op("pool", lambda g, e0=e0, zf32=zf32: g.dma_start(
                        out=xg_hbm[e0].rearrange("(n p) h -> p n h", p=128),
                        in_=zf32[:].rearrange("p (n h) -> p n h", n=NST)),
                        reads=["zsrc"], writes=[f"xgh.{e0}"],
                        dma=True, accum_write=True)
            ldconst(c_alpha, d_alpha, "alpha")

            for w, (start, end) in enumerate(wins):
                L = end - start
                nkt = (L + 127) // 128
                ov = 0 if w == 0 else 256
                tbs = [f"qT.{0}.{b}" for b in range(start // 512, (end - 1) // 512 + 1)]
                for hp in range(8):
                    qkeys = [f"qT.{hp}.{b}" for b in range(start // 512, (end - 1) // 512 + 1)]
                    kkeys = [f"kT.{hp}.{b}" for b in range(start // 512, (end - 1) // 512 + 1)]
                    for kt in range(nkt):
                        ktL = min(128, L - 128 * kt)
                        pk = kt % 2
                        po = 2 + kt % 2
                        c0 = S * hp + start
                        P.op("pe", lambda t, kt=kt, ktL=ktL, pk=pk, c0=c0, L=L:
                             t.matmul(psb[pk][0:ktL, 0:L],
                                      kT[0:64, c0 + 128 * kt:c0 + 128 * kt + ktL],
                                      qT[0:64, c0:c0 + L], start=True, stop=True),
                             reads=kkeys + qkeys, writes=[f"pb{pk}"])
                        P.op("pe", lambda t, kt=kt, ktL=ktL, po=po, c0=c0, L=L:
                             t.matmul(psb[po][0:ktL, 0:L],
                                      kT[64:128, c0 + 128 * kt:c0 + 128 * kt + ktL],
                                      qT[64:128, c0:c0 + L], start=True, stop=True),
                             reads=kkeys + qkeys, writes=[f"pb{po}"])
                        P.op("act", lambda sc, kt=kt, ktL=ktL, pk=pk, L=L:
                             sc.activation(expE[0:ktL, 512 * kt:512 * kt + L],
                                           psb[pk][0:ktL, 0:L], AF.Exp, scale=scale),
                             reads=[f"pb{pk}"], writes=[f"expE.{kt}"])
                        P.op("act", lambda sc, kt=kt, ktL=ktL, po=po, L=L:
                             sc.activation(expO[0:ktL, 512 * kt:512 * kt + L],
                                           psb[po][0:ktL, 0:L], AF.Exp, scale=scale),
                             reads=[f"pb{po}"], writes=[f"expO.{kt}"])
                    for par, expT, ekey in ((0, expE, "expE"), (1, expO, "expO")):
                        for kt in range(nkt):
                            ktL = min(128, L - 128 * kt)
                            tti = (start + 128 * kt) // 128
                            hcol = 1024 * tti + 64 * (2 * hp + par)
                            tp = (0, 64 * par)
                            P.op("pe", lambda t, kt=kt, ktL=ktL, tti=tti, hcol=hcol,
                                 expT=expT, par=par, L=L, nkt=nkt, tp=tp:
                                 t.matmul(psb[4][64 * par:64 * par + 64, 0:L],
                                          v_tm[0:ktL, hcol:hcol + 64],
                                          expT[0:ktL, 512 * kt:512 * kt + L],
                                          start=(kt == 0), stop=(kt == nkt - 1),
                                          tile_position=tp),
                                 reads=[f"v.{tti}", f"{ekey}.{kt}"], writes=["pb4"])
                    for par, expT, ekey in ((0, expE, "expE"), (1, expO, "expO")):
                        for kt in range(nkt):
                            ktL = min(128, L - 128 * kt)
                            tp = (0, 32 * par)
                            P.op("pe", lambda t, kt=kt, ktL=ktL, par=par, expT=expT,
                                 L=L, nkt=nkt, tp=tp:
                                 t.matmul(psb[5][32 * par:32 * par + 1, 0:L],
                                          ones[0:ktL, 0:1],
                                          expT[0:ktL, 512 * kt:512 * kt + L],
                                          start=(kt == 0), stop=(kt == nkt - 1),
                                          tile_position=tp),
                                 reads=["ones", f"{ekey}.{kt}"], writes=["pb5"])
                    for par in range(2):
                        r = 32 * par
                        P.op("act", lambda sc, r=r, L=L: sc.activation(
                            lnr[r:r + 1, 0:L], psb[5][r:r + 1, 0:L], AF.Ln),
                            reads=["pb5"], writes=[f"lnr{par}"])
                    for par in range(2):
                        r = 32 * par
                        P.op("act", lambda sc, r=r, L=L: sc.activation(
                            wrow[r:r + 1, 0:L], lnr[r:r + 1, 0:L], AF.Exp,
                            scale=-1.0), reads=[f"lnr{par}"], writes=[f"wrow{par}"])
                    for par in range(2):
                        r = 32 * par
                        P.op("dve", lambda v, r=r, L=L, w=w: v.tensor_tensor(
                            wrow[r:r + 1, 0:L], wrow[r:r + 1, 0:L],
                            c_alpha[r:r + 1, 512 * w:512 * w + L], ALU.mult),
                            reads=[f"wrow{par}", "alpha"], writes=[f"wrow{par}"])
                    for par in range(2):
                        r = 32 * par
                        tp = (32 * par, 64 * par)
                        P.op("pe", lambda t, r=r, L=L, par=par, tp=tp: t.matmul(
                            psb[6][64 * par:64 * par + 64, 0:L],
                            ones[r:r + 1, 0:64], wrow[r:r + 1, 0:L],
                            start=True, stop=True, tile_position=tp),
                            reads=["ones", f"wrow{par}"], writes=["pb6"])
                    P.op("dve", lambda v, L=L: v.tensor_copy(
                        wb_sb[:, 0:L], psb[6][:, 0:L]),
                        reads=["pb6"], writes=["wb"])
                    cc = S * hp + start
                    P.op("dve", lambda v, cc=cc, ov=ov, L=L: v.tensor_tensor(
                        ctxT[:, cc + ov:cc + L], psb[4][:, ov:L],
                        wb_sb[:, ov:L], ALU.mult),
                        reads=["pb4", "wb"], writes=[f"ctx.{hp}.{w}"])
                    if w > 0:
                        P.op("dve", lambda v, cc=cc: v.tensor_tensor(
                            tmp256[:], psb[4][:, 0:256], wb_sb[:, 0:256], ALU.mult),
                            reads=["pb4", "wb"], writes=["tmp256"])
                        P.op("dve", lambda v, cc=cc: v.tensor_tensor(
                            ctxT[:, cc:cc + 256], ctxT[:, cc:cc + 256],
                            tmp256[:], ALU.add),
                            reads=["tmp256", f"ctx.{hp}.{w - 1}"],
                            writes=[f"ctx.{hp}.{w}", f"ctx.{hp}.{w - 1}"])
            ctx_keys = [f"ctx.{hp}.{w}" for hp in range(8) for w in range(NW)]
            tap("ctx" + sfx, ctxT[:], ctx_keys)

            P.barrier()
            ld_w1024(wbufP, wproj, "wbufP")
            ldconst(c_bproj, d_bproj, "bproj")
            ldconst(c_b2c, d_b2, "b2c")
            P.op("dve", lambda v: v.tensor_add(c_pb2[:], c_bproj[:], c_b2c[:]),
                 reads=["bproj", "b2c"], writes=["pb2"])
            for tt in range(NT):
                b = tt % 2
                P.op("act", lambda sc, b=b, tt=tt, s=s: sc.dma_start(
                    out=x0t2[b][:], in_=x[s, 128 * tt:128 * tt + 128, :]),
                    writes=[f"x0u{b}"], dma=True)
                P.op("dve", lambda v, b=b: v.tensor_add(
                    xbA[b][:], x0t2[b][:], c_bproj[:]),
                    reads=[f"x0u{b}", "bproj"], writes=[f"xbA{b}"])
                P.op("dve", lambda v, b=b: v.scalar_tensor_tensor(
                    x0t2[b][:], x0t2[b][:], 2.0, c_pb2[:], ALU.mult, ALU.add),
                    reads=[f"x0u{b}", "pb2"], writes=[f"x0u{b}"])
                for ob in range(2):
                    pk = nxpb(2)
                    for kk in range(8):
                        P.op("pe", lambda t, tt=tt, ob=ob, kk=kk, pk=pk:
                             t.matmul(psb[pk][:],
                                      ctxT[:, S * kk + 128 * tt:S * kk + 128 * tt + 128],
                                      wbufP[:, 1024 * kk + 512 * ob:1024 * kk + 512 * ob + 512],
                                      start=(kk == 0), stop=(kk == 7)),
                             reads=["wbufP", f"ctx.{kk}.{NW - 1}", f"ctx.{kk}.{NW - 2}"],
                             writes=[f"pb{pk}"])
                    P.op("dve", lambda v, tt=tt, ob=ob, pk=pk, b=b: v.tensor_tensor(
                        rS[:, 1024 * tt + 512 * ob:1024 * tt + 512 * ob + 512],
                        psb[pk][:], x0t2[b][:, 512 * ob:512 * ob + 512], ALU.add),
                        reads=[f"pb{pk}", f"x0u{b}"], writes=[f"rS.{tt}.{ob}"])
                    P.op("dve", lambda v, tt=tt, ob=ob, pk=pk, b=b: v.tensor_tensor(
                        o_bf[:, 1024 * tt + 512 * ob:1024 * tt + 512 * ob + 512],
                        psb[pk][:], xbA[b][:, 512 * ob:512 * ob + 512], ALU.add),
                        reads=[f"pb{pk}", f"xbA{b}"], writes=[f"obf.{tt}"])
            tap("o" + sfx, o_bf[:], [f"obf.{tt}" for tt in range(NT)])

            P.barrier()
            ldconst(c_ln2g, d_ln2g, "ln2g")
            ldconst(c_ln2b, d_ln2b, "ln2b")

            for tb in range(NTB):
                for ti in range(4):
                    tt = 4 * tb + ti
                    ln_pass1(o_bf[:, 1024 * tt:1024 * tt + 1024], [f"obf.{tt}"],
                             sq2, "sq2", ti, "E")
                ln_finish(4, "E")
                for ti in range(4):
                    tt = 4 * tb + ti
                    okey = [f"obf.{tt}"]
                    oap = o_bf[:, 1024 * tt:1024 * tt + 1024]
                    mcol, rcol = ln_mr(ti)
                    P.op("dve", lambda v, oap=oap, mcol=mcol, rcol=rcol: v.tensor_scalar(
                        sq2[:], oap, mcol, rcol,
                        ALU.subtract, ALU.mult),
                        reads=okey + ["E.m", "E.r"], writes=["sq2"])
                    P.op("dve", lambda v: v.tensor_tensor(
                        sq2[:], sq2[:], c_ln2g[:], ALU.mult),
                        reads=["sq2", "ln2g"], writes=["sq2"])
                    P.op("dve", lambda v: v.tensor_tensor(
                        xn2tm[:], sq2[:], c_ln2b[:], ALU.add),
                        reads=["sq2", "ln2b"], writes=["xn2tm"])
                    for j in range(8):
                        P.op("pe", lambda t, j=j: t.transpose(
                            pbT[:, 128 * j:128 * j + 128],
                            xn2tm[:, 128 * j:128 * j + 128], ident[:]),
                            reads=["xn2tm", "ident"], writes=["pbT"])
                    P.op("dve", lambda v, ti=ti: v.tensor_copy(
                        xn2T[:].rearrange("p (j t) -> p j t", j=8)[:, :, 128 * ti:128 * ti + 128],
                        pbT[:].rearrange("p (j t) -> p j t", j=8)),
                        reads=["pbT"], writes=[f"xn2T.{ti}"])
                for c in range(8):
                    wkey = f"w1c.{c % 2}"
                    P.op("pool", lambda g, c=c, buf=w1buf[c % 2]: g.dma_start(
                        out=buf[:].rearrange("p (kk i) -> p kk i", kk=8),
                        in_=w1.rearrange("(kk p) i -> p kk i", p=128)[:, :, 512 * c:512 * c + 512]),
                        writes=[wkey], dma=True)
                    for i4 in range(4):
                        ii = 4 * c + i4
                        pk = nxpb(4)
                        for kk in range(8):
                            P.op("pe", lambda t, kk=kk, i4=i4, pk=pk, buf=w1buf[c % 2]:
                                 t.matmul(psb[pk][:],
                                          buf[:, 512 * kk + 128 * i4:512 * kk + 128 * i4 + 128],
                                          xn2T[:, 512 * kk:512 * kk + 512],
                                          start=(kk == 0), stop=(kk == 7)),
                                 reads=[wkey] + [f"xn2T.{i}" for i in range(4)],
                                 writes=[f"pb{pk}"])
                        P.op("act", lambda sc, ii=ii, pk=pk: sc.activation(
                            h1T[:, 512 * ii:512 * ii + 512], psb[pk][:], AF.Gelu,
                            bias=c_b1[:, ii:ii + 1]),
                            reads=[f"pb{pk}", "c_b1"], writes=[f"h1.{ii}"])
                if NTB == 1:
                    tap("xn2" + sfx, xn2T[:], [f"xn2T.{i}" for i in range(4)])
                    tap("h1" + sfx, h1T[:], [f"h1.{ii}" for ii in range(32)])
                for oh in range(2):
                    for a in range(8):
                        wkey = f"w2c.{a % 2}"
                        P.op("pool", lambda g, a=a, oh=oh, buf=w2buf[a % 2]: g.dma_start(
                            out=buf[:].rearrange("p (kk o) -> p kk o", kk=4),
                            in_=w2.rearrange("(kk p) o -> p kk o", p=128)[:, 4 * a:4 * a + 4, 512 * oh:512 * oh + 512]),
                            writes=[wkey], dma=True)
                        for ti in range(4):
                            for k4 in range(4):
                                kk = 4 * a + k4
                                P.op("pe", lambda t, kk=kk, k4=k4, ti=ti, a=a,
                                     buf=w2buf[a % 2]:
                                     t.matmul(psb[ti][:],
                                              h1T[:, 512 * kk + 128 * ti:512 * kk + 128 * ti + 128],
                                              buf[:, 512 * k4:512 * k4 + 512],
                                              start=(kk == 0), stop=(kk == 31)),
                                     reads=[wkey, f"h1.{kk}"], writes=[f"pb{ti}"])
                    for ti in range(4):
                        tt = 4 * tb + ti
                        P.op("dve", lambda v, tt=tt, oh=oh, ti=ti: v.tensor_tensor(
                            rS[:, 1024 * tt + 512 * oh:1024 * tt + 512 * oh + 512],
                            rS[:, 1024 * tt + 512 * oh:1024 * tt + 512 * oh + 512],
                            psb[ti][:], ALU.add),
                            reads=[f"pb{ti}", f"rS.{tt}.{oh}"], writes=[f"rS.{tt}.{oh}"])
            tap("S2" + sfx, rS[:], [f"rS.{tt}.{ob}" for tt in range(NT) for ob in range(2)])

            P.barrier()
            for tt in range(NT):
                b = tt % 2
                rkeys = [f"rS.{tt}.0", f"rS.{tt}.1"]
                rap = rS[:, 1024 * tt:1024 * tt + 1024]
                ln_pass1(rap, rkeys, sq3, "sq3", tt, "F")
            ln_finish(NT, "F")
            for tt in range(NT):
                b = tt % 2
                rkeys = [f"rS.{tt}.0", f"rS.{tt}.1"]
                rap = rS[:, 1024 * tt:1024 * tt + 1024]
                mcol, rcol = ln_mr(tt)
                P.op("dve", lambda v, rap=rap, mcol=mcol, rcol=rcol: v.tensor_scalar(
                    rap, rap, mcol, rcol, ALU.subtract, ALU.mult),
                    reads=rkeys + ["F.m", "F.r"], writes=rkeys)
                P.op("dve", lambda v, rap=rap, b=b: v.tensor_copy(x1tmp[b][:], rap),
                     reads=rkeys, writes=[f"x1t{b}"])
                for j in range(8):
                    P.op("pe", lambda t, b=b, j=j: t.transpose(
                        pbT[:, 128 * j:128 * j + 128],
                        x1tmp[b][:, 128 * j:128 * j + 128], ident[:]),
                        reads=[f"x1t{b}", "ident"], writes=["pbT"])
                P.op("dve", lambda v, tt=tt: v.tensor_copy(
                    x1T[:].rearrange("p (j t) -> p j t", j=8)[:, :, 128 * tt:128 * tt + 128],
                    pbT[:].rearrange("p (j t) -> p j t", j=8)),
                    reads=["pbT"], writes=[f"x1T.{tt}"])
            tap("x1" + sfx, rS[:], [f"rS.{tt}.{ob}" for tt in range(NT) for ob in range(2)])

            for tt in range(NT):
                for kk in range(8):
                    P.op("pe", lambda t, tt=tt, kk=kk: t.matmul(
                        psb[6][:, 0:8],
                        x1T[:, S * kk + 128 * tt:S * kk + 128 * tt + 128],
                        c_rt[:, 8 * kk:8 * kk + 8],
                        start=(kk == 0), stop=(kk == 7)),
                        reads=[f"x1T.{tt}", "c_rt"], writes=["pb6"])
                P.op("dve", lambda v: v.tensor_copy(lgt[:], psb[6][:, 0:8]),
                     reads=["pb6"], writes=["lgt"])
                P.op("dve", lambda v: v.tensor_reduce(
                    stats[:, 8:9], lgt[:], AX.X, ALU.max),
                    reads=["lgt"], writes=["g.m1"])
                P.op("dve", lambda v: v.tensor_scalar(
                    eq1[:], lgt[:], stats[:, 8:9], None, ALU.is_equal),
                    reads=["lgt", "g.m1"], writes=["g.eq1"])
                P.op("dve", lambda v: v.scalar_tensor_tensor(
                    lmsk[:], eq1[:], -1e30, lgt[:], ALU.mult, ALU.add),
                    reads=["g.eq1", "lgt"], writes=["g.lm"])
                P.op("dve", lambda v: v.tensor_reduce(
                    stats[:, 9:10], lmsk[:], AX.X, ALU.max),
                    reads=["g.lm"], writes=["g.m2"])
                P.op("dve", lambda v: v.tensor_sub(
                    stats[:, 10:11], stats[:, 8:9], stats[:, 9:10]),
                    reads=["g.m1", "g.m2"], writes=["g.d"])
                P.op("act", lambda sc: sc.activation(
                    stats[:, 11:12], stats[:, 10:11], AF.Sigmoid),
                    reads=["g.d"], writes=["g.s1"])
                P.op("dve", lambda v: v.tensor_scalar(
                    stats[:, 12:13], stats[:, 11:12], -1.0, 1.0, ALU.mult, ALU.add),
                    reads=["g.s1"], writes=["g.w2"])
                P.op("dve", lambda v: v.tensor_scalar(
                    eq2[:], lgt[:], stats[:, 9:10], None, ALU.is_equal),
                    reads=["lgt", "g.m2"], writes=["g.eq2"])
                if moe_sparse:
                    P.op("dve", lambda v: v.tensor_add(maskt[:], eq1[:], eq2[:]),
                         reads=["g.eq1", "g.eq2"], writes=["g.mask"])
                P.op("dve", lambda v: v.tensor_scalar_mul(
                    t1t[:], eq1[:], stats[:, 11:12]),
                    reads=["g.eq1", "g.s1"], writes=["g.t1"])
                P.op("dve", lambda v: v.tensor_scalar_mul(
                    eq2[:], eq2[:], stats[:, 12:13]),
                    reads=["g.eq2", "g.w2"], writes=["g.eq2"])
                P.op("dve", lambda v, tt=tt: v.tensor_add(
                    c_sb[:, 8 * tt:8 * tt + 8], t1t[:], eq2[:]),
                    reads=["g.t1", "g.eq2"], writes=[f"comb.{tt}"])
                if moe_sparse:
                    if tt == 0:
                        P.op("dve", lambda v: v.memset(baserow[0:1, 0:8], -1.0),
                             writes=["gbase"])
                    P.op("dve", lambda v: v.tensor_copy(maskb[:], maskt[:]),
                         reads=["g.mask"], writes=["g.maskb"])
                    P.op("pe", lambda t: t.matmul(
                        psb[3][:, 0:8], c_onesf[0:1, 0:128], baserow[0:1, 0:8],
                        start=True, stop=True),
                        reads=["c_onesf", "gbase"], writes=["pb3"])
                    P.op("pe", lambda t: t.matmul(
                        psb[5][:, 0:8], c_tri[:], maskb[:],
                        start=True, stop=True),
                        reads=["c_tri", "g.maskb"], writes=["pb5"])
                    P.op("pe", lambda t: t.matmul(
                        psb[4][0:1, 0:8], ones[:, 0:1], maskb[:],
                        start=True, stop=True),
                        reads=["ones", "g.maskb"], writes=["pb4"])
                    P.op("dve", lambda v: v.scalar_tensor_tensor(
                        slotf[:], maskt[:], -65536.0, psb[5][:, 0:8],
                        ALU.mult, ALU.add),
                        reads=["g.mask", "pb5"], writes=["g.slotf"])
                    P.op("dve", lambda v: v.tensor_tensor(
                        slotf[:], slotf[:], psb[3][:, 0:8], ALU.add),
                        reads=["g.slotf", "pb3"], writes=["g.slotf"])
                    P.op("dve", lambda v: v.tensor_scalar(
                        slotf[:], slotf[:], 1.0, 65536.0, ALU.mult, ALU.add),
                        reads=["g.slotf"], writes=["g.slotf"])
                    P.op("dve", lambda v: v.tensor_scalar(
                        okt[:], slotf[:], float(CAP), None, ALU.is_lt),
                        reads=["g.slotf"], writes=["g.ok"])
                    P.op("dve", lambda v, tt=tt: v.tensor_tensor(
                        c_sb[:, 8 * tt:8 * tt + 8], c_sb[:, 8 * tt:8 * tt + 8],
                        okt[:], ALU.mult),
                        reads=[f"comb.{tt}", "g.ok"], writes=[f"comb.{tt}"])
                    P.op("dve", lambda v, tt=tt: v.tensor_copy(
                        sloti_sb[:, 8 * tt:8 * tt + 8], slotf[:]),
                        reads=["g.slotf"], writes=[f"sloti.{tt}"])
                    P.op("dve", lambda v: v.tensor_add(
                        baserow[0:1, 0:8], baserow[0:1, 0:8], psb[4][0:1, 0:8]),
                        reads=["gbase", "pb4"], writes=["gbase"])
                    bx2 = tt % 2
                    P.op("dve", lambda v, tt=tt, bx2=bx2: v.tensor_copy(
                        xsrc[bx2][:], rS[:, 1024 * tt:1024 * tt + 1024]),
                        reads=[f"rS.{tt}.0", f"rS.{tt}.1"], writes=[f"xsrc{bx2}"])
                    for e2 in range(E_EXP):
                        P.op("pool", lambda g, tt=tt, e2=e2, bx2=bx2:
                             g.indirect_dma_start(
                                 xg_hbm.rearrange("e c h -> (e c) h"),
                                 IndirectOffsetOnAxis(
                                     sloti_sb[:, 8 * tt + e2:8 * tt + e2 + 1], 0),
                                 xsrc[bx2][:], None,
                                 element_offset=e2 * CAP * HID,
                                 bounds_check=bc_reg(g), oob_is_err=False),
                             reads=[f"xsrc{bx2}", f"sloti.{tt}"],
                             writes=[f"xgh.{e2}"], dma=True, accum_write=True)
            tap("combine" + sfx, c_sb[:], [f"comb.{tt}" for tt in range(NT)])

            P.barrier()
            if moe_sparse:
                def emit_rb(e):
                    for st in range(NST):
                        bb = st % 2
                        P.op("pool", lambda g, e=e, st=st, bb=bb: g.dma_start(
                            out=xgt[bb][:],
                            in_=xg_hbm[e, 128 * st:128 * st + 128, :]),
                            reads=[f"xgh.{e}"], writes=[f"xgt{bb}"], dma=True)
                        for j in range(8):
                            P.op("pe", lambda t, bb=bb, j=j: t.transpose(
                                pbT[:, 128 * j:128 * j + 128],
                                xgt[bb][:, 128 * j:128 * j + 128], ident[:]),
                                reads=[f"xgt{bb}", "ident"], writes=["pbT"])
                        P.op("dve", lambda v, st=st: v.tensor_copy(
                            xgT[:].rearrange("p (j t) -> p j t", j=8)[:, :, 128 * st:128 * st + 128],
                            pbT[:].rearrange("p (j t) -> p j t", j=8)),
                            reads=["pbT"], writes=[f"xgT.{st}"])

                def emit_addback(e_src, tt):
                    bg = tt % 2
                    P.op("pool", lambda g, e=e_src, tt=tt, bg=bg: g.indirect_dma_start(
                        ygbuf[bg][:], None,
                        yb_hbm.rearrange("e c h -> (e c) h"),
                        IndirectOffsetOnAxis(
                            sloti_sb[:, 8 * tt + e:8 * tt + e + 1], 0),
                        element_offset=e * CAP * HID,
                        bounds_check=bc_reg(g), oob_is_err=False),
                        reads=[f"yb.{e_src}.{st}" for st in range(NST)]
                        + [f"sloti.{tt}"], writes=[f"ygb{bg}"], dma=True)
                    P.op("dve", lambda v, tt=tt, e=e_src, bg=bg: v.scalar_tensor_tensor(
                        rS[:, 1024 * tt:1024 * tt + 1024],
                        ygbuf[bg][:], c_sb[:, 8 * tt + e:8 * tt + e + 1],
                        rS[:, 1024 * tt:1024 * tt + 1024],
                        ALU.mult, ALU.add),
                        reads=[f"ygb{bg}", f"comb.{tt}", f"rS.{tt}.0", f"rS.{tt}.1"],
                        writes=[f"rS.{tt}.0", f"rS.{tt}.1"])

                pending = []
                emit_rb(0)
                for e in range(E_EXP):
                    for half in range(2):
                        i_base = half * (NIH * 128)
                        for ii in range(NIH):
                            b = ii % NWB
                            if pending:
                                emit_addback(*pending.pop(0))
                            prefetched = (e > 0 and half == 0 and ii < 2)
                            if ii == 2:
                                for ii2 in range(NIH):
                                    P.op("pool", lambda g, e=e, ii2=ii2, i_base=i_base: g.dma_start(
                                        out=wd_sb[:, 1024 * ii2:1024 * ii2 + 1024],
                                        in_=wd_t[e, i_base + 128 * ii2:i_base + 128 * ii2 + 128, :]),
                                        writes=[f"wd.{ii2}"], dma=True)
                            if not prefetched:
                                P.op("pool", lambda g, e=e, ii=ii, b=b, i_base=i_base: g.dma_start(
                                    out=wgbuf[b][:].rearrange("p (kk i) -> p kk i", kk=8),
                                    in_=wg_t.rearrange("e (kk p) i -> e p kk i", p=128)[e][:, :, i_base + 128 * ii:i_base + 128 * ii + 128]),
                                    writes=[f"wgb{b}"], dma=True)
                                P.op("pool", lambda g, e=e, ii=ii, b=b, i_base=i_base: g.dma_start(
                                    out=wubuf[b][:].rearrange("p (kk i) -> p kk i", kk=8),
                                    in_=wu_t.rearrange("e (kk p) i -> e p kk i", p=128)[e][:, :, i_base + 128 * ii:i_base + 128 * ii + 128]),
                                    writes=[f"wub{b}"], dma=True)
                            for tb in range(NTBC):
                                pg = tb % 2
                                pu = 2 + tb % 2
                                sb_ = tb % 2
                                xkeys = [f"xgT.{(TBW * tb) // 128 + i}"
                                         for i in range(TBW // 128)]
                                for kk in range(8):
                                    P.op("pe", lambda t, kk=kk, tb=tb, b=b, pg=pg:
                                         t.matmul(psb[pg][:, 0:TBW],
                                                  wgbuf[b][:, 128 * kk:128 * kk + 128],
                                                  xgT[:, CAP * kk + TBW * tb:CAP * kk + TBW * tb + TBW],
                                                  start=(kk == 0), stop=(kk == 7)),
                                         reads=[f"wgb{b}"] + xkeys, writes=[f"pb{pg}"])
                                P.op("act", lambda sc, pg=pg, sb_=sb_: sc.activation(
                                    silu_t[sb_][:, 0:TBW], psb[pg][:, 0:TBW], AF.Silu),
                                    reads=[f"pb{pg}"], writes=[f"silu{sb_}"])
                                for kk in range(8):
                                    P.op("pe", lambda t, kk=kk, tb=tb, b=b, pu=pu:
                                         t.matmul(psb[pu][:, 0:TBW],
                                                  wubuf[b][:, 128 * kk:128 * kk + 128],
                                                  xgT[:, CAP * kk + TBW * tb:CAP * kk + TBW * tb + TBW],
                                                  start=(kk == 0), stop=(kk == 7)),
                                         reads=[f"wub{b}"] + xkeys, writes=[f"pb{pu}"])
                                P.op("dve", lambda v, ii=ii, tb=tb, pu=pu, sb_=sb_: v.tensor_tensor(
                                    h_half[:, CAP * ii + TBW * tb:CAP * ii + TBW * tb + TBW],
                                    silu_t[sb_][:, 0:TBW], psb[pu][:, 0:TBW], ALU.mult),
                                    reads=[f"silu{sb_}", f"pb{pu}"], writes=[f"hh.{ii}"])
                        for st in range(NST):
                            for oh in range(2):
                                pk = 4 + (2 * st + oh) % 3
                                for kk in range(NIH):
                                    P.op("pe", lambda t, kk=kk, st=st, oh=oh, pk=pk:
                                         t.matmul(psb[pk][:],
                                                  h_half[:, CAP * kk + 128 * st:CAP * kk + 128 * st + 128],
                                                  wd_sb[:, 1024 * kk + 512 * oh:1024 * kk + 512 * oh + 512],
                                                  start=(kk == 0), stop=(kk == NIH - 1)),
                                         reads=[f"hh.{kk}", f"wd.{kk}"], writes=[f"pb{pk}"])
                                if half == 0:
                                    P.op("act", lambda sc, st=st, oh=oh, pk=pk: sc.activation(
                                        ysum[:, 1024 * st + 512 * oh:1024 * st + 512 * oh + 512],
                                        psb[pk][:], AF.Copy),
                                        reads=[f"pb{pk}"], writes=[f"ys.{st}.{oh}"])
                                else:
                                    P.op("dve", lambda v, st=st, oh=oh, pk=pk: v.tensor_tensor(
                                        ysum[:, 1024 * st + 512 * oh:1024 * st + 512 * oh + 512],
                                        ysum[:, 1024 * st + 512 * oh:1024 * st + 512 * oh + 512],
                                        psb[pk][:], ALU.add),
                                        reads=[f"pb{pk}", f"ys.{st}.{oh}"],
                                        writes=[f"ys.{st}.{oh}"])
                            if half == 1:
                                P.op("act", lambda sc, e=e, st=st: sc.dma_start(
                                    out=yb_hbm[e, 128 * st:128 * st + 128, :],
                                    in_=ysum[:, 1024 * st:1024 * st + 1024]),
                                    reads=[f"ys.{st}.0", f"ys.{st}.1"],
                                    writes=[f"yb.{e}.{st}"], dma=True)
                    if e + 1 < E_EXP:
                        emit_rb(e + 1)
                        for ii in range(2):
                            P.op("pool", lambda g, e1=e + 1, ii=ii: g.dma_start(
                                out=wgbuf[ii][:].rearrange("p (kk i) -> p kk i", kk=8),
                                in_=wg_t.rearrange("e (kk p) i -> e p kk i", p=128)[e1][:, :, 128 * ii:128 * ii + 128]),
                                writes=[f"wgb{ii}"], dma=True)
                            P.op("pool", lambda g, e1=e + 1, ii=ii: g.dma_start(
                                out=wubuf[ii][:].rearrange("p (kk i) -> p kk i", kk=8),
                                in_=wu_t.rearrange("e (kk p) i -> e p kk i", p=128)[e1][:, :, 128 * ii:128 * ii + 128]),
                                writes=[f"wub{ii}"], dma=True)
                    pending.extend((e, tt) for tt in range(NT))
                for es, tt in pending:
                    emit_addback(es, tt)
                pending = []
            if not moe_sparse:
              for e in range(E_EXP):
                for half in range(2):
                      i_base = half * (NIH * 128)
                      for ii in range(NIH):
                          b = ii % 4
                          if ii == 2:
                              for ii2 in range(NIH):
                                  P.op("pool", lambda g, e=e, ii2=ii2, i_base=i_base: g.dma_start(
                                      out=wd_sb[:, 1024 * ii2:1024 * ii2 + 1024],
                                      in_=wd_t[e, i_base + 128 * ii2:i_base + 128 * ii2 + 128, :]),
                                      writes=[f"wd.{ii2}"], dma=True)
                          P.op("pool", lambda g, e=e, ii=ii, b=b, i_base=i_base: g.dma_start(
                              out=wgbuf[b][:].rearrange("p (kk i) -> p kk i", kk=8),
                              in_=wg_t.rearrange("e (kk p) i -> e p kk i", p=128)[e][:, :, i_base + 128 * ii:i_base + 128 * ii + 128]),
                              writes=[f"wgb{b}"], dma=True)
                          P.op("pool", lambda g, e=e, ii=ii, b=b, i_base=i_base: g.dma_start(
                              out=wubuf[b][:].rearrange("p (kk i) -> p kk i", kk=8),
                              in_=wu_t.rearrange("e (kk p) i -> e p kk i", p=128)[e][:, :, i_base + 128 * ii:i_base + 128 * ii + 128]),
                              writes=[f"wub{b}"], dma=True)
                          for tb in range(NTB):
                              pg = tb % 2
                              pu = 2 + tb % 2
                              xkeys = [f"x1T.{4 * tb + i}" for i in range(4)]
                              for kk in range(8):
                                  P.op("pe", lambda t, kk=kk, tb=tb, b=b, pg=pg:
                                       t.matmul(psb[pg][:],
                                                wgbuf[b][:, 128 * kk:128 * kk + 128],
                                                x1T[:, S * kk + 512 * tb:S * kk + 512 * tb + 512],
                                                start=(kk == 0), stop=(kk == 7)),
                                       reads=[f"wgb{b}"] + xkeys, writes=[f"pb{pg}"])
                              sb_ = tb % 2
                              P.op("act", lambda sc, pg=pg, sb_=sb_: sc.activation(
                                  silu_t[sb_][:], psb[pg][:], AF.Silu),
                                  reads=[f"pb{pg}"], writes=[f"silu{sb_}"])
                              for kk in range(8):
                                  P.op("pe", lambda t, kk=kk, tb=tb, b=b, pu=pu:
                                       t.matmul(psb[pu][:],
                                                wubuf[b][:, 128 * kk:128 * kk + 128],
                                                x1T[:, S * kk + 512 * tb:S * kk + 512 * tb + 512],
                                                start=(kk == 0), stop=(kk == 7)),
                                       reads=[f"wub{b}"] + xkeys, writes=[f"pb{pu}"])
                              P.op("dve", lambda v, ii=ii, tb=tb, pu=pu, sb_=sb_: v.tensor_tensor(
                                  h_half[:, S * ii + 512 * tb:S * ii + 512 * tb + 512],
                                  silu_t[sb_][:], psb[pu][:], ALU.mult),
                                  reads=[f"silu{sb_}", f"pb{pu}"], writes=[f"hh.{ii}"])
                      for tt in range(NT):
                          for oh in range(2):
                              pk = 4 + (2 * tt + oh) % 3
                              for kk in range(NIH):
                                  P.op("pe", lambda t, kk=kk, tt=tt, oh=oh, pk=pk:
                                       t.matmul(psb[pk][:],
                                                h_half[:, S * kk + 128 * tt:S * kk + 128 * tt + 128],
                                                wd_sb[:, 1024 * kk + 512 * oh:1024 * kk + 512 * oh + 512],
                                                start=(kk == 0), stop=(kk == NIH - 1)),
                                       reads=[f"hh.{kk}", f"wd.{kk}"], writes=[f"pb{pk}"])
                              P.op("dve", lambda v, tt=tt, oh=oh, pk=pk, e=e:
                                   v.scalar_tensor_tensor(
                                       rS[:, 1024 * tt + 512 * oh:1024 * tt + 512 * oh + 512],
                                       psb[pk][:], c_sb[:, 8 * tt + e:8 * tt + e + 1],
                                       rS[:, 1024 * tt + 512 * oh:1024 * tt + 512 * oh + 512],
                                       ALU.mult, ALU.add),
                                   reads=[f"pb{pk}", f"comb.{tt}", f"rS.{tt}.{oh}"],
                                   writes=[f"rS.{tt}.{oh}"])
            tap("premoe_out" + sfx, rS[:],
                [f"rS.{tt}.{ob}" for tt in range(NT) for ob in range(2)])

            P.barrier()
            for tt in range(NT):
                rkeys = [f"rS.{tt}.0", f"rS.{tt}.1"]
                rap = rS[:, 1024 * tt:1024 * tt + 1024]
                ln_pass1(rap, rkeys, sq4, "sq4", tt, "I")
            ln_finish(NT, "I")
            for tt in range(NT):
                b = tt % 2
                rkeys = [f"rS.{tt}.0", f"rS.{tt}.1"]
                rap = rS[:, 1024 * tt:1024 * tt + 1024]
                mcol, rcol = ln_mr(tt)
                P.op("dve", lambda v, rap=rap, b=b, mcol=mcol, rcol=rcol: v.tensor_scalar(
                    outb[b][:], rap, mcol, rcol,
                    ALU.subtract, ALU.mult),
                    reads=rkeys + ["I.m", "I.r"], writes=[f"outb{b}"])
                P.op("sp", lambda sp, tt=tt, b=b, s=s: sp.dma_start(
                    out=y[s, 128 * tt:128 * tt + 128, :], in_=outb[b][:]),
                    reads=[f"outb{b}"], dma=True)
            P.barrier()

        P.op("sp", lambda sp: sp.nop(),
             writes=[f"outb{b}" for b in range(2)])

        block = ctx.enter_context(nc.Block())
        P.emit(block, sems)
    return nc


def _prep_inputs(kw, shard_x):
    bf = ml_dtypes.bfloat16
    rep = lambda a: np.ascontiguousarray(
        np.broadcast_to(np.asarray(a, np.float32), (128, a.shape[-1])))
    S = shard_x.shape[1]
    m = {
        "x": np.ascontiguousarray(shard_x, dtype=np.float32),
        "wq": np.ascontiguousarray(kw["L0_attn_wq"], np.float32),
        "wk": np.ascontiguousarray(kw["L0_attn_wk"], np.float32),
        "wv": np.ascontiguousarray(kw["L0_attn_wv"], np.float32),
        "wproj": np.ascontiguousarray(kw["L0_attn_wproj"], np.float32),
        "w1": np.ascontiguousarray(kw["L0_attn_w1"], np.float32),
        "w2": np.ascontiguousarray(kw["L0_attn_w2"], np.float32),
        "router_t": np.ascontiguousarray(kw["L0_ffn_router_w"].T, np.float32),
        "wg_t": np.ascontiguousarray(np.asarray(kw["L0_ffn_Wg"], np.float32).transpose(0, 2, 1)),
        "wu_t": np.ascontiguousarray(np.asarray(kw["L0_ffn_Wu"], np.float32).transpose(0, 2, 1)),
        "wd_t": np.ascontiguousarray(np.asarray(kw["L0_ffn_Wd"], np.float32).transpose(0, 2, 1)),
        "c_ident": np.eye(128, dtype=np.float32).astype(bf),
        "c_ones": np.ones((128, 128), dtype=np.float32).astype(bf),
        "c_alpha": alpha_table(S),
        "c_ln1g": rep(kw["L0_attn_ln1_g"]),
        "c_ln1b": rep(kw["L0_attn_ln1_b"]),
        "c_ln2g": rep(kw["L0_attn_ln2_g"]),
        "c_ln2b": rep(kw["L0_attn_ln2_b"]),
        "c_bv": rep(kw["L0_attn_bv"]),
        "c_bproj": rep(kw["L0_attn_bproj"]),
        "c_b2": rep(kw["L0_attn_b2"]),
        "c_bq": np.ascontiguousarray(
            np.asarray(kw["L0_attn_bq"], np.float32).reshape(8, 128).T),
        "c_bk": np.ascontiguousarray(
            np.asarray(kw["L0_attn_bk"], np.float32).reshape(8, 128).T),
        "c_b1": np.ascontiguousarray(
            np.asarray(kw["L0_attn_b1"], np.float32).reshape(32, 128).T),
        "c_tri": np.ascontiguousarray(np.triu(np.ones((128, 128), np.float32)).astype(ml_dtypes.bfloat16)),
        "c_onesf": np.ones((128, 128), np.float32),
    }
    return m


def kernel(**kw):
    x_full = np.asarray(kw["L0_attn_hidden_states"], np.float32)
    B, S, _ = x_full.shape
    n_cores = 8
    per = B // n_cores
    nc = build_program(S=S, NSEQ=per, moe_sparse=True, CAP=S // 2)
    in_maps = [_prep_inputs(kw, x_full[c * per:(c + 1) * per]) for c in range(n_cores)]
    res = run_bass_kernel_spmd(nc, in_maps, core_ids=list(range(n_cores)))
    return np.concatenate([np.asarray(r["y"]) for r in res.results], axis=0)
